# version 27
# baseline (speedup 1.0000x reference)
_LASTP = None
_LASTM = None
"""Fused single-launch megakernel for mk_prb_d1 (LN -> SSM + MoE, residual).

out = x + SSM(LN(x)) + MoE(LN(x)),  x: (2, 8192, 2048) fp32.

Sharding: 16384 tokens -> 8 contiguous 2048-token blocks (each inside one
sequence).  Only the SSM scan carry crosses cores (AllGather of 8x64 f32).

On-chip layout is feature-major ([d on partitions, tokens on free]); the host
pre-transposes x and pre-blocks weights (layout only) so every DMA is
contiguous; the device writes the output feature-major and the host
transposes it back.

All matmuls use bf16 operands (cast during DMA / on op output) with fp32 PSUM
accumulation; elementwise math is fp32; LN statistics use float32r matmuls.
MoE is computed densely per expert with per-token gates
g_e(t) = w1*[l==m1] + w2*[l==m2], w1 = sigmoid(m1-m2) (exactly the
reference's masked math).  The SSM diagonal scan uses the rotation trick:
u_l = e^{-i phi l} s_l gives a real per-channel-decay scan, executed with
tensor_tensor_scan.  Output accumulation (init residual+D-term, y2, expert
partials) uses DMA accumulate-add into HBM.
"""

import contextlib
import numpy as np
import ml_dtypes
import concourse.bass as bass
import concourse.mybir as mybir
from concourse.bass_utils import run_bass_kernel_spmd

F32 = mybir.dt.float32
F32R = mybir.dt.float32r
BF16 = mybir.dt.bfloat16
I32 = mybir.dt.int32
U8 = mybir.dt.uint8
AF = mybir.ActivationFunctionType
ALU = mybir.AluOpType
AX = mybir.AxisListType

D = 2048
TOK = 2048
NCH = 16
IKCH = 8
NE = 8
HN = 32
TAU = 512
NTQ = 4
HB = 1024
TWOPI = float(2.0 * np.pi)
INV2PI = float(1.0 / (2.0 * np.pi))
PIHALF = float(np.pi / 2.0)
PIF = float(np.pi)

ENGS = ("sync", "gpsimd", "vector", "scalar", "tensor")


class Prog:
    def __init__(self):
        self.q = {e: [] for e in ENGS}
        self.cnt = {}

    def sem(self, name):
        self.cnt[name] = 0

    AUTOSEM = {"vector": "VE", "scalar": "AC"}

    def op(self, eng, fn, waits=(), inc="auto", amt=None, ser=True):
        if inc == "auto":
            inc = self.AUTOSEM.get(eng)
        if amt is None:
            amt = 1
        waits = list(waits)
        if ser and eng in self.AUTOSEM:
            s = self.AUTOSEM[eng]
            if self.cnt[s] > 0:
                waits.append((s, self.cnt[s]))
        seq = sum(len(v) for v in self.q.values())
        self.q[eng].append((tuple(waits), fn, inc, amt, seq))
        if inc is not None:
            self.cnt[inc] += amt
            return self.cnt[inc]
        return None

    def c(self, sem):
        return self.cnt[sem]


def build_nc(debug=False):
    nc = bass.Bass("TRN2", target_bir_lowering=False, debug=False)

    dX = nc.dram_tensor("x_fm", [NCH, 2, 128, HB], F32,
                        kind="ExternalInput").ap()
    dWG = nc.dram_tensor("wg", [NE, IKCH, 128, NCH, 128], F32,
                         kind="ExternalInput").ap()
    dWU = nc.dram_tensor("wu", [NE, IKCH, 128, NCH, 128], F32,
                         kind="ExternalInput").ap()
    dWD = nc.dram_tensor("wd", [NE, NCH, 128, IKCH, 128], F32,
                         kind="ExternalInput").ap()
    dRG = nc.dram_tensor("rgt", [128, 128], F32, kind="ExternalInput").ap()
    dBRE = nc.dram_tensor("bre", [HN, D], F32, kind="ExternalInput").ap()
    dBIM = nc.dram_tensor("bim", [HN, D], F32, kind="ExternalInput").ap()
    dCRE = nc.dram_tensor("cret", [HN, D], F32, kind="ExternalInput").ap()
    dCIM = nc.dram_tensor("cimt", [HN, D], F32, kind="ExternalInput").ap()
    dPRM = nc.dram_tensor("prm", [1, 96], F32, kind="ExternalInput").ap()
    dDV = nc.dram_tensor("dvec", [128, 16], F32, kind="ExternalInput").ap()
    dIOTA = nc.dram_tensor("iota64", [2, 64, HB], F32,
                           kind="ExternalInput").ap()
    dID128 = nc.dram_tensor("id128", [128, 128], F32, kind="ExternalInput").ap()
    dID8 = nc.dram_tensor("id8", [8, 8], F32, kind="ExternalInput").ap()
    dID1 = nc.dram_tensor("id1", [1, 1], F32, kind="ExternalInput").ap()
    dID32B = nc.dram_tensor("id32b", [64, 32], BF16, kind="ExternalInput").ap()
    dONESRC = nc.dram_tensor("onesr_col", [128, 1], F32R,
                             kind="ExternalInput").ap()
    dONESRR = nc.dram_tensor("onesr_row", [1, 128], F32R,
                             kind="ExternalInput").ap()
    dONES8 = nc.dram_tensor("ones8", [8, 1], F32, kind="ExternalInput").ap()
    dOH = nc.dram_tensor("oh", [8, NE * 128], F32R, kind="ExternalInput").ap()
    dSGNV = nc.dram_tensor("sgnv", [64, 1], F32, kind="ExternalInput").ap()
    dSGNX = nc.dram_tensor("sgnx", [64, 1], F32, kind="ExternalInput").ap()
    dCVEC = nc.dram_tensor("cvec", [128, 8], F32, kind="ExternalInput").ap()
    dDIST = nc.dram_tensor("dist8", [8, 1], F32, kind="ExternalInput").ap()
    dMASK = nc.dram_tensor("mask8", [8, 1], F32, kind="ExternalInput").ap()

    dOUT = nc.dram_tensor("out_fm", [NCH, 2, 128, HB], F32,
                          kind="ExternalOutput").ap()
    dCCI = nc.dram_tensor("cc_in", [1, 64], F32).ap()
    dCCO = nc.dram_tensor("cc_out", [8, 64], F32).ap()

    dbg = {}
    if debug:
        for nm, shp, dt in [("dbg_bu", [64, TOK], F32),
                            ("dbg_u", [64, TOK], F32),
                            ("dbg_xs", [64, TOK], F32),
                            ("dbg_lg", [8, TOK], F32),
                            ("dbg_gfm", [8, TOK], F32),
                            ("dbg_mr", [1, TOK], F32),
                            ("dbg_rr", [1, TOK], F32),
                            ("dbg_xn", [128, NCH * TOK], BF16),
                            ("dbg_h", [128, IKCH * TOK], BF16),
                            ("dbg_ek", [8, 64], F32),
                            ("dbg_uin", [64, 1], F32),
                            ("dbg_pcol", [64, 24], F32),
                            ("dbg_prow", [1, 512], F32),
                            ("dbg_bst", [128, 1024], BF16),
                            ("dbg_carr", [8, 512], F32)]:
            dbg[nm] = nc.dram_tensor(nm, shp, dt, kind="ExternalOutput").ap()

    xn_t = nc.alloc_sbuf_tensor("xn_t", [128, NCH * TOK], BF16)
    xs_t = nc.alloc_sbuf_tensor("xs_t", [64, TOK], BF16)
    cstk = nc.alloc_sbuf_tensor("cstk", [64, D], BF16)
    bstk = nc.alloc_sbuf_tensor("bstk", [128, NCH * 64], BF16)
    gfm = nc.alloc_sbuf_tensor("gfm", [8, TOK], F32R)
    carr = nc.alloc_sbuf_tensor("carr", [8, 512], F32)
    pcol = nc.alloc_sbuf_tensor("pcol", [64, 24], F32)
    prow = nc.alloc_sbuf_tensor("prow", [1, 512], F32)
    srg = nc.alloc_sbuf_tensor("srg", [128, 128], BF16)
    sid128 = nc.alloc_sbuf_tensor("sid128", [128, 128], F32)
    sid8 = nc.alloc_sbuf_tensor("sid8", [8, 8], F32)
    sid1 = nc.alloc_sbuf_tensor("sid1", [1, 1], F32)
    sid32b = nc.alloc_sbuf_tensor("sid32b", [64, 32], BF16)
    sorc = nc.alloc_sbuf_tensor("sorc", [128, 1], F32R)
    sorr = nc.alloc_sbuf_tensor("sorr", [1, 128], F32R)
    so8 = nc.alloc_sbuf_tensor("so8", [8, 1], F32)
    soh = nc.alloc_sbuf_tensor("soh", [8, NE * 128], F32R)
    ssgv = nc.alloc_sbuf_tensor("ssgv", [64, 1], F32)
    ssgx = nc.alloc_sbuf_tensor("ssgx", [64, 1], F32)
    scv = nc.alloc_sbuf_tensor("scv", [128, 8], F32)
    sdv = nc.alloc_sbuf_tensor("sdv", [128, 16], F32)
    sdst = nc.alloc_sbuf_tensor("sdst", [8, 1], F32)
    smsk = nc.alloc_sbuf_tensor("smsk", [8, 1], F32)
    sprm = nc.alloc_sbuf_tensor("sprm", [1, 96], F32)
    zcol = nc.alloc_sbuf_tensor("zcol", [64, 1], F32)
    ucol = nc.alloc_sbuf_tensor("ucol", [64, 1], F32)
    ecol = nc.alloc_sbuf_tensor("ecol", [64, 4], F32)
    uinc = nc.alloc_sbuf_tensor("uinc", [64, 1], F32)

    OVR_KB = 114
    ovr = nc.alloc_sbuf_tensor("OVR", [128, OVR_KB * 1024], U8)
    base = nc.lookup_mloc(ovr).addr

    def at(name, shape, dt, off_kb):
        return nc.alloc_sbuf_tensor_at(name, shape, dt,
                                       offset=base + int(off_kb * 1024))

    x_sc = at("x_sc", [128, 2 * HB], F32, 0)
    xr_sc = at("xr_sc", [128, 2 * HB], F32R, 8)
    xq_sc = at("xq_sc", [128, 2 * HB], F32R, 16)
    xn32_sc = at("xn32_sc", [128, 2 * HB], F32, 24)
    oi_sc = at("oi_sc", [128, 2 * HB], F32, 32)
    rows = at("rows", [1, 4096], F32, 40)
    rowr = at("rowr", [1, 2048], F32R, 56)
    lg_sb = at("lg_sb", [8, TOK], F32, 64)
    ltm = at("ltm", [128, 128], F32, 72)
    t2s = at("t2s", [128, 640], F32, 72.5)
    wg_r = at("wg_r", [128, 2 * NCH * 128], BF16, 0)
    wu_r = at("wu_r", [128, 2 * NCH * 128], BF16, 8)
    wd_r = at("wd_r", [128, 3 * IKCH * 128], BF16, 16)
    s_sc = at("s_sc", [128, 2 * TAU], F32, 22)
    m_sb = at("m_sb", [128, HB], F32, 75)
    r_sb = at("r_sb", [128, HB], F32, 79)
    g_sb = at("g_sb", [128, TOK], BF16, 30)
    g_sbo = at("g_sbo", [128, TOK], BF16, 34)
    gsc = at("gsc", [128, 4 * HB], F32, 38)
    h_t = at("h_t", [128, IKCH * TOK], BF16, 54)
    sbu = at("sbu", [64, TOK], F32, 86)
    sswp = at("sswp", [64, HB], F32, 94)
    svv = at("svv", [64, HB], F32, 98)
    scos = at("scos", [64, HB], F32, 102)
    ssin = at("ssin", [64, HB], F32, 106)
    ssc5 = at("ssc5", [64, HB], F32, 110)

    pb = [nc.alloc_psum_tensor("pb%d" % i, [128, 512], F32) for i in range(7)]
    pb7b = nc.alloc_psum_tensor("pb7b", [128, 1024], BF16)
    pb7f = pb7b.ap().bitcast(F32)

    p = Prog()
    global _LASTP, _LASTM
    _LASTP = p
    for s in ("XL", "XC", "XI", "XA0", "XA1", "WG0", "WG1", "WU0", "WU1", "WD0", "WD1", "WD2", "SG0", "SG1", "SG2", "SG3", "SI0", "SI1", "XR0", "XR1", "CCL", "GL", "SO", "CC", "PE", "VE", "AC", "EW"):
        p.sem(s)
    M = {}
    _LASTM = M

    def seqmark(tag):
        M[("seq", tag)] = sum(len(v) for v in p.q.values())

    def dma(eng, dst, src, inc, waits=(), slow=False, accum=False):
        if accum:
            fn = (lambda e, a=dst, b=src: e.dma_start(out=a, in_=b,
                                                      accum_op=ALU.add))
        else:
            fn = (lambda e, a=dst, b=src, s=slow:
                  e.dma_start(out=a, in_=b, allow_slow_non_contiguous=s))
        return p.op(eng, fn, waits=waits, inc=inc, amt=16)

    for dst, src, slow in [
            (sprm[:, :], dPRM, False), (scv[:, :], dCVEC, False),
            (sdv[:, :], dDV, False), (sid128[:, :], dID128, False),
            (sid8[:, :], dID8, False), (sid1[:, :], dID1, False),
            (sid32b[:, :], dID32B, False),
            (sorc[:, :], dONESRC, False), (sorr[:, :], dONESRR, False),
            (so8[:, :], dONES8, False), (ssgv[:, :], dSGNV, False),
            (ssgx[:, :], dSGNX, False), (sdst[:, :], dDIST, False),
            (smsk[:, :], dMASK, False),
            (sbu[0:HN, 0:D], dBRE, False),
            (ssc5[0:HN, :], dBIM[:, 0:HB], False),
            (ssc5[HN:64, :], dBIM[:, HB:D], False)]:
        dma("sync", dst, src, "XL", slow=slow)
    M["p0_sync"] = p.c("XL")
    dma("gpsimd", srg[:, :], dRG, "GL")
    dma("gpsimd", soh[:, :], dOH, "GL")
    M["p0_gl"] = p.c("GL")

    def R(i):
        return prow[:, 32 * i:32 * (i + 1)]

    lam_u = sprm[:, 0:32]
    lam_im = sprm[:, 32:64]
    logd = sprm[:, 64:96]
    one_ap = scv[0:1, 1:2]
    wXL = [("XL", M["p0_sync"])]

    p.op("scalar", lambda e: e.activation(R(0), lam_u, AF.Abs), waits=wXL)
    p.op("scalar", lambda e: e.activation(R(1), R(0), AF.Exp, scale=-1.0))
    p.op("scalar", lambda e: e.activation(R(2), R(1), AF.Ln, bias=one_ap))
    p.op("scalar", lambda e: e.activation(R(5), logd, AF.Exp), inc="AC")
    M["sp0"] = p.c("AC")
    p.op("vector", lambda e: e.tensor_scalar_max(R(3), lam_u, 0.0),
         waits=wXL + [("AC", M["sp0"])])
    p.op("vector", lambda e: e.tensor_add(R(0), R(3), R(2)))
    p.op("vector", lambda e: e.tensor_scalar(R(4), R(0), -1.0, -0.0101,
                                             ALU.mult, ALU.add))
    p.op("vector", lambda e: e.tensor_mul(R(6), R(4), R(5)))
    p.op("vector", lambda e: e.tensor_mul(R(7), lam_im, R(5)))
    p.op("vector", lambda e: e.tensor_mul(R(9), R(4), R(4)))
    p.op("vector", lambda e: e.tensor_mul(R(10), lam_im, lam_im))
    p.op("vector", lambda e: e.tensor_add(R(9), R(9), R(10)))
    p.op("vector", lambda e: e.reciprocal(R(10), R(9)), inc="VE")
    M["pch_v"] = p.c("VE")
    p.op("scalar", lambda e: e.activation(R(8), R(6), AF.Exp),
         waits=[("VE", M["pch_v"])], inc="AC")
    M["pch_a"] = p.c("AC")

    def C(i, n=1, p0=0, np_=32):
        return pcol[p0:p0 + np_, i:i + n]

    prev_ev = [0]

    def row2col(row_ap, col_i, extra=()):
        w = [("VE", prev_ev[0])] if prev_ev[0] else []
        p.op("tensor", (lambda e, a=row_ap: e.transpose(
            pb[0][0:32, 0:1], a, sid1[:, :])),
            waits=list(extra) + w, inc="PE")
        t = p.c("PE")
        p.op("vector", (lambda e, ci=col_i: e.tensor_copy(
            pcol[0:32, ci:ci + 1], pb[0][0:32, 0:1])),
            waits=[("PE", t)], inc="VE")
        prev_ev[0] = p.c("VE")

    wp = [("VE", M["pch_v"]), ("AC", M["pch_a"]), ("XL", M["p0_sync"])]
    row2col(R(6), 0, wp)
    row2col(R(7), 1)
    row2col(R(8), 2)
    row2col(R(4), 3)
    row2col(lam_im, 4)
    row2col(R(10), 5)
    M["pcols"] = prev_ev[0]

    p.op("vector", lambda e: e.tensor_scalar_mul(R(12), R(7), INV2PI),
         waits=[("VE", M["pcols"])])
    p.op("vector", lambda e: e.tensor_copy(R(13).bitcast(I32), R(12)))
    p.op("vector", lambda e: e.tensor_copy(R(12), R(13).bitcast(I32)))
    p.op("vector", lambda e: e.scalar_tensor_tensor(
        R(13), R(12), -TWOPI, R(7), ALU.mult, ALU.add))
    p.op("vector", lambda e: e.tensor_single_scalar(R(11), R(13), PIF,
                                                    ALU.is_gt))
    p.op("vector", lambda e: e.scalar_tensor_tensor(
        R(13), R(11), -TWOPI, R(13), ALU.mult, ALU.add))
    p.op("vector", lambda e: e.tensor_single_scalar(R(11), R(13), -PIF,
                                                    ALU.is_lt))
    p.op("vector", lambda e: e.scalar_tensor_tensor(
        R(13), R(11), TWOPI, R(13), ALU.mult, ALU.add))
    v1 = p.c("VE")
    p.op("scalar", lambda e: e.activation(R(14), R(13), AF.Sin),
         waits=[("VE", v1)])
    p.op("vector", lambda e: e.tensor_scalar_add(R(12), R(7), PIHALF),
         waits=[("AC", p.c("AC"))])
    p.op("vector", lambda e: e.tensor_scalar_mul(R(11), R(12), INV2PI))
    p.op("vector", lambda e: e.tensor_copy(R(13).bitcast(I32), R(11)))
    p.op("vector", lambda e: e.tensor_copy(R(11), R(13).bitcast(I32)))
    p.op("vector", lambda e: e.scalar_tensor_tensor(
        R(13), R(11), -TWOPI, R(12), ALU.mult, ALU.add))
    p.op("vector", lambda e: e.tensor_single_scalar(R(12), R(13), PIF,
                                                    ALU.is_gt))
    p.op("vector", lambda e: e.scalar_tensor_tensor(
        R(13), R(12), -TWOPI, R(13), ALU.mult, ALU.add))
    p.op("vector", lambda e: e.tensor_single_scalar(R(12), R(13), -PIF,
                                                    ALU.is_lt))
    p.op("vector", lambda e: e.scalar_tensor_tensor(
        R(13), R(12), TWOPI, R(13), ALU.mult, ALU.add))
    v2 = p.c("VE")
    p.op("scalar", lambda e: e.activation(R(15), R(13), AF.Sin),
         waits=[("VE", v2)])
    M["cossin"] = p.c("AC")
    row2col(R(14), 7, [("AC", M["cossin"])])
    row2col(R(15), 6, [("AC", M["cossin"])])
    M["cossin_v"] = prev_ev[0]

    wcs = [("VE", M["cossin_v"])]
    p.op("vector", lambda e: e.tensor_mul(C(12), C(2), C(6)), waits=wcs)
    p.op("vector", lambda e: e.tensor_scalar_add(C(12), C(12), -1.0))
    p.op("vector", lambda e: e.tensor_mul(C(13), C(2), C(7)))
    p.op("vector", lambda e: e.tensor_mul(C(14), C(12), C(3)))
    p.op("vector", lambda e: e.tensor_mul(C(15), C(13), C(4)))
    p.op("vector", lambda e: e.tensor_add(C(14), C(14), C(15)))
    p.op("vector", lambda e: e.tensor_mul(C(8), C(14), C(5)))
    p.op("vector", lambda e: e.tensor_mul(C(14), C(13), C(3)))
    p.op("vector", lambda e: e.tensor_mul(C(15), C(12), C(4)))
    p.op("vector", lambda e: e.tensor_sub(C(14), C(14), C(15)))
    p.op("vector", lambda e: e.tensor_mul(C(9), C(14), C(5)))
    p.op("vector", lambda e: e.tensor_scalar_mul(C(10), C(9), -1.0))
    p.op("vector", lambda e: e.tensor_copy(pcol[0:32, 16:17], C(2)))
    p.op("vector", lambda e: e.tensor_copy(pcol[32:64, 16:17], C(2)))
    p.op("vector", lambda e: e.tensor_copy(pcol[0:32, 17:18], C(0)))
    p.op("vector", lambda e: e.tensor_copy(pcol[32:64, 17:18], C(0)))
    p.op("vector", lambda e: e.tensor_copy(pcol[0:32, 18:19], C(1)))
    p.op("vector", lambda e: e.tensor_copy(pcol[32:64, 18:19], C(1)))
    p.op("vector", lambda e: e.memset(zcol[:, :], 0.0), inc="VE")
    M["ccoef"] = p.c("VE")
    RRC = pcol[0:64, 16:17]
    RHOD = pcol[0:64, 17:18]
    PHID = pcol[0:64, 18:19]

    p.op("vector", lambda e: e.tensor_copy(svv[0:32, :], ssc5[32:64, :]),
         waits=[("XL", M["p0_sync"]), ("VE", M["ccoef"])])
    p.op("vector", lambda e: e.tensor_scalar(sswp[0:32, :], sbu[0:32, 0:HB],
                                             C(8), None, ALU.mult))
    p.op("vector", lambda e: e.scalar_tensor_tensor(
        sswp[0:32, :], ssc5[0:32, :], C(10), sswp[0:32, :],
        ALU.mult, ALU.add))
    p.op("vector", lambda e: e.tensor_scalar(scos[0:32, :], ssc5[0:32, :],
                                             C(8), None, ALU.mult))
    p.op("vector", lambda e: e.scalar_tensor_tensor(
        scos[0:32, :], sbu[0:32, 0:HB], C(9), scos[0:32, :],
        ALU.mult, ALU.add))
    p.op("vector", lambda e: e.tensor_scalar(ssin[0:32, :], sbu[0:32, HB:D],
                                             C(8), None, ALU.mult))
    p.op("vector", lambda e: e.scalar_tensor_tensor(
        ssin[0:32, :], svv[0:32, :], C(10), ssin[0:32, :],
        ALU.mult, ALU.add))
    p.op("vector", lambda e: e.tensor_scalar(ssc5[0:32, :], svv[0:32, :],
                                             C(8), None, ALU.mult))
    p.op("vector", lambda e: e.scalar_tensor_tensor(
        ssc5[0:32, :], sbu[0:32, HB:D], C(9), ssc5[0:32, :],
        ALU.mult, ALU.add))
    p.op("vector", lambda e: e.tensor_copy(xs_t[0:32, 0:HB], sswp[0:32, :]))
    p.op("vector", lambda e: e.tensor_copy(xs_t[0:32, HB:D], ssin[0:32, :]))
    p.op("vector", lambda e: e.tensor_copy(xs_t[32:64, 0:HB], scos[0:32, :]))
    p.op("vector", lambda e: e.tensor_copy(xs_t[32:64, HB:D], ssc5[0:32, :]),
         inc="VE")
    M["bbar"] = p.c("VE")

    prev = M["bbar"]
    for k in range(NCH):
        for half in range(2):
            src = xs_t[0:32, 128 * k:128 * (k + 1)] if half == 0 \
                else xs_t[32:64, 128 * k:128 * (k + 1)]
            idn = sid32b[0:32, :] if half == 0 else sid32b[32:64, :]
            coff = 64 * k + 32 * half
            p.op("tensor", (lambda e, src=src, idn=idn: e.transpose(
                pb7b.ap()[0:128, 0:32], src, idn)),
                waits=[("VE", prev)], inc="PE")
            t = p.c("PE")
            p.op("vector", (lambda e, coff=coff: e.tensor_copy(
                bstk[:, coff:coff + 32], pb7b.ap()[0:128, 0:32])),
                waits=[("PE", t)], inc="VE")
            prev = p.c("VE")
    M["bstack"] = prev

    dma("sync", sswp[0:32, :], dCRE[:, 0:HB], "XC",
        waits=[("VE", M["bstack"])])
    dma("sync", ssin[0:32, :], dCRE[:, HB:D], "XC")
    dma("sync", svv[0:32, :], dCIM[:, 0:HB], "XC")
    dma("sync", ssc5[0:32, :], dCIM[:, HB:D], "XC")
    M["cload"] = p.c("XC")
    p.op("vector", lambda e: e.tensor_scalar_mul(cstk[0:32, 0:HB],
                                                 sswp[0:32, :], 2.0),
         waits=[("XC", M["cload"])])
    p.op("vector", lambda e: e.tensor_scalar_mul(cstk[0:32, HB:D],
                                                 ssin[0:32, :], 2.0))
    p.op("vector", lambda e: e.tensor_scalar_mul(cstk[32:64, 0:HB],
                                                 svv[0:32, :], -2.0))
    p.op("vector", lambda e: e.tensor_scalar_mul(cstk[32:64, HB:D],
                                                 ssc5[0:32, :], -2.0),
         inc="VE")
    M["cstack"] = p.c("VE")

    seqmark("p0end")
    s_row = rows[:, 0:1024]
    q_row = rows[:, 1024:2048]
    t_row = rows[:, 2048:3072]
    v_row = rows[:, 3072:4096]
    m_r = rowr[:, 0:1024]
    r_r = rowr[:, 1024:2048]

    region_so = {}
    xr_pe = [0, 0]
    xr_ac = [0, 0]
    xr_ve = [0, 0]
    xsl_ve = [0, 0]
    xn_ac = [0, 0]
    oi_so = [None, None]

    def emit_passA_chunk(hf, c):
        hoff = HB * hf
        sl = c % 2
        so = sl * HB
        if hf == 0:
            wD = []
            if xr_ve[sl]:
                wD.append(("VE", xr_ve[sl]))
            if xsl_ve[sl]:
                wD.append(("VE", xsl_ve[sl]))
            dma("sync", x_sc[:, so:so + HB],
                dX[c, hf], "XA%d" % sl,
                waits=wD)
            xl = p.c("XA%d" % sl)
            wC = [("XA%d" % sl, xl)]
            if xr_pe[sl]:
                wC.append(("PE", xr_pe[sl]))
            p.op("vector", (lambda e, so=so: e.tensor_copy(
                xr_sc[:, so:so + HB], x_sc[:, so:so + HB])), waits=wC)
            xr_ve[sl] = p.c("VE")
            wS = [("VE", xr_ve[sl])]
        else:
            wD = []
            if xr_pe[sl]:
                wD.append(("PE", xr_pe[sl]))
            if xr_ac[sl]:
                wD.append(("AC", xr_ac[sl]))
            dma("gpsimd", xr_sc[:, so:so + HB],
                dX[c, hf], "XR%d" % sl,
                waits=wD)
            wS = [("XR%d" % sl, p.c("XR%d" % sl))]
        if xr_pe[sl]:
            wS.append(("PE", xr_pe[sl]))
        p.op("scalar", (lambda e, so=so: e.activation(
            xq_sc[:, so:so + HB], xr_sc[:, so:so + HB], AF.Square)),
            waits=wS, inc="AC")
        xr_ac[sl] = p.c("AC")
        wM = list(wS) + [("AC", xr_ac[sl])]
        if hf == 0 and c == 0:
            wM.append(("VE", M["cstack"]))
        if hf == 1 and c == 0:
            wM.append(("VE", M[("mrrr", 0)]))
        for tq in range(2):
            p.op("tensor", (lambda e, so=so, tq=tq, c=c: e.matmul(
                pb[tq][0:1, :], sorc[:, :],
                xr_sc[:, so + TAU * tq:so + TAU * (tq + 1)],
                start=(c == 0), stop=(c == NCH - 1))),
                waits=(wM if tq == 0 else []))
            p.op("tensor", (lambda e, so=so, tq=tq, c=c: e.matmul(
                pb[2 + tq][0:1, :], sorc[:, :],
                xq_sc[:, so + TAU * tq:so + TAU * (tq + 1)],
                start=(c == 0), stop=(c == NCH - 1))),
                inc=("PE" if tq == 1 else None))
        xr_pe[sl] = p.c("PE")

    def emit_stats_math(hf):
        M[("statmm_end", hf)] = p.c("PE")
        wst = [("PE", M[("statmm_end", hf)])]
        p.op("vector", lambda e: e.tensor_copy(s_row[:, 0:TAU],
                                               pb[0][0:1, :]), waits=wst)
        p.op("vector", lambda e: e.tensor_copy(s_row[:, TAU:HB],
                                               pb[1][0:1, :]))
        p.op("vector", lambda e: e.tensor_copy(q_row[:, 0:TAU],
                                               pb[2][0:1, :]))
        p.op("vector", lambda e: e.tensor_copy(q_row[:, TAU:HB],
                                               pb[3][0:1, :]))
        p.op("vector", lambda e: e.tensor_scalar_mul(s_row, s_row, 1.0 / D))
        p.op("vector", lambda e: e.tensor_mul(t_row, s_row, s_row))
        p.op("vector", lambda e: e.tensor_scalar_mul(v_row, q_row, 1.0 / D))
        p.op("vector", lambda e: e.tensor_sub(v_row, v_row, t_row))
        vvar = p.c("VE")
        p.op("scalar", lambda e: e.activation(t_row, v_row, AF.Ln,
                                              bias=scv[0:1, 0:1]),
             waits=[("VE", vvar)])
        p.op("scalar", lambda e: e.activation(v_row, t_row, AF.Exp,
                                              scale=-0.5), inc="AC")
        arstd = p.c("AC")
        p.op("vector", lambda e: e.tensor_copy(m_r, s_row),
             waits=[("AC", arstd)])
        p.op("vector", lambda e: e.tensor_copy(r_r, v_row))
        M[("mrrr", hf)] = p.c("VE")
        wbc = [("VE", M[("mrrr", hf)])]
        if hf == 0:
            wbc.append(("VE", M["bstack"]))
        else:
            wbc.append(("VE", M[("p1end", 0)]))
        p.op("tensor", lambda e: e.matmul(pb[4][:, :], sorr[:, :],
                                          m_r[:, 0:TAU], start=True,
                                          stop=True), waits=wbc)
        p.op("tensor", lambda e: e.matmul(pb[5][:, :], sorr[:, :],
                                          m_r[:, TAU:HB], start=True,
                                          stop=True))
        p.op("tensor", lambda e: e.matmul(pb[6][:, :], sorr[:, :],
                                          r_r[:, 0:TAU], start=True,
                                          stop=True))
        p.op("tensor", lambda e: e.matmul(pb7f[:, :], sorr[:, :],
                                          r_r[:, TAU:HB], start=True,
                                          stop=True), inc="PE")
        M[("bcast", hf)] = p.c("PE")
        p.op("vector", lambda e: e.tensor_copy(m_sb[:, 0:TAU], pb[4][:, :]),
             waits=[("PE", M[("bcast", hf)])])
        p.op("vector", lambda e: e.tensor_copy(m_sb[:, TAU:HB], pb[5][:, :]))
        p.op("vector", lambda e: e.tensor_copy(r_sb[:, 0:TAU], pb[6][:, :]))
        p.op("vector", lambda e: e.tensor_copy(r_sb[:, TAU:HB], pb7f[:, :]))
        M[("bcastv", hf)] = p.c("VE")

    def emit_passB_chunk(hf, c):
        hoff = HB * hf
        sl = c % 2
        so = sl * HB
        wD = []
        if xsl_ve[sl]:
            wD.append(("VE", xsl_ve[sl]))
        if hf == 0 and xr_ve[sl]:
            wD.append(("VE", xr_ve[sl]))
        dma("sync", x_sc[:, so:so + HB],
            dX[c, hf], "XA%d" % sl,
            waits=wD)
        xl = p.c("XA%d" % sl)
        wv = [("XA%d" % sl, xl), ("VE", M[("bcastv", hf)])]
        if xn_ac[sl]:
            wv.append(("AC", xn_ac[sl]))
        if oi_so[sl]:
            wv.append(oi_so[sl])
        p.op("vector", (lambda e, so=so: e.tensor_tensor(
            xn32_sc[:, so:so + HB], x_sc[:, so:so + HB],
            m_sb[:, :], ALU.subtract)), waits=wv)
        p.op("vector", (lambda e, so=so: e.tensor_tensor(
            xn32_sc[:, so:so + HB], xn32_sc[:, so:so + HB],
            r_sb[:, :], ALU.mult)))
        p.op("vector", (lambda e, so=so, c=c: e.scalar_tensor_tensor(
            oi_sc[:, so:so + HB], xn32_sc[:, so:so + HB],
            sdv[:, c:c + 1], x_sc[:, so:so + HB], ALU.mult, ALU.add)))
        xsl_ve[sl] = p.c("VE")
        M[("xn32", hf, c)] = xsl_ve[sl]
        p.op("scalar", (lambda e, so=so, c=c, hoff=hoff: e.activation(
            xn_t[:, TOK * c + hoff:TOK * c + hoff + HB],
            xn32_sc[:, so:so + HB], AF.Copy)),
            waits=[("VE", xsl_ve[sl])], inc="AC")
        xn_ac[sl] = p.c("AC")
        M[("xnbf", hf, c)] = xn_ac[sl]
        p.op("scalar", (lambda e, so=so, c=c, hf=hf: e.dma_start(
            out=dOUT[c, hf],
            in_=oi_sc[:, so:so + HB])), inc="SI%d" % sl, amt=16)
        oi_so[sl] = ("SI%d" % sl, p.c("SI%d" % sl))
        region_so[(c, hf)] = oi_so[sl]
        wmm = [("AC", xn_ac[sl])]
        if c == 0:
            wmm.append(("GL", M["p0_gl"]))
            wmm.append(("VE", M[("bcastv", hf)]))
        for tq in range(2):
            p.op("tensor", (lambda e, c=c, tq=tq, hoff=hoff: e.matmul(
                pb[4 + tq][0:64, :], bstk[:, 64 * c:64 * (c + 1)],
                xn_t[:, TOK * c + hoff + TAU * tq:
                     TOK * c + hoff + TAU * (tq + 1)],
                start=(c == 0), stop=(c == NCH - 1))),
                waits=(wmm if tq == 0 else []))
            p.op("tensor", (lambda e, c=c, tq=tq, hoff=hoff: e.matmul(
                (pb[6] if tq == 0 else pb7f)[0:8, :],
                srg[:, 8 * c:8 * (c + 1)],
                xn_t[:, TOK * c + hoff + TAU * tq:
                     TOK * c + hoff + TAU * (tq + 1)],
                start=(c == 0), stop=(c == NCH - 1))),
                inc=("PE" if tq == 1 else None))
        M[("bulg", hf, c)] = p.c("PE")

    def emit_p1end(hf):
        hoff = HB * hf
        M[("bulg_end", hf)] = p.c("PE")
        wbu = [("PE", M[("bulg_end", hf)])]
        p.op("vector", (lambda e, hoff=hoff: e.tensor_copy(
            sbu[:, hoff:hoff + TAU], pb[4][0:64, :])), waits=wbu)
        p.op("vector", (lambda e, hoff=hoff: e.tensor_copy(
            sbu[:, hoff + TAU:hoff + HB], pb[5][0:64, :])))
        p.op("vector", (lambda e, hoff=hoff: e.tensor_copy(
            lg_sb[:, hoff:hoff + TAU], pb[6][0:8, :])))
        p.op("vector", (lambda e, hoff=hoff: e.tensor_copy(
            lg_sb[:, hoff + TAU:hoff + HB], pb7f[0:8, :])))
        M[("p1end", hf)] = p.c("VE")

    def tables_gen(hf, sgn_ap, pre, guard=None):
        hoff = HB * hf
        if guard is None:
            guard = [("VE", p.c("VE"))]
        dma("sync", sswp[:, :], dIOTA[hf], "XI",
            waits=list(pre) + list(guard))
        xl = p.c("XI")
        yield
        p.op("vector", lambda e: e.tensor_scalar(svv[:, :], sswp[:, :],
                                                 PHID, None, ALU.mult),
             waits=[("XI", xl)] + list(pre))
        yield
        p.op("vector", lambda e: e.tensor_scalar_mul(ssin[:, :], svv[:, :],
                                                     INV2PI))
        yield
        p.op("vector", lambda e: e.tensor_copy(ssc5.ap().bitcast(I32),
                                               ssin[:, :]))
        yield
        p.op("vector", lambda e: e.tensor_copy(ssin[:, :],
                                               ssc5.ap().bitcast(I32)))
        yield
        p.op("vector", lambda e: e.scalar_tensor_tensor(
            svv[:, :], ssin[:, :], -TWOPI, svv[:, :], ALU.mult, ALU.add))
        yield
        p.op("vector", lambda e: e.tensor_single_scalar(
            ssc5[:, :], svv[:, :], PIF, ALU.is_gt))
        yield
        p.op("vector", lambda e: e.scalar_tensor_tensor(
            svv[:, :], ssc5[:, :], -TWOPI, svv[:, :], ALU.mult, ALU.add))
        yield
        p.op("vector", lambda e: e.tensor_single_scalar(
            ssc5[:, :], svv[:, :], -PIF, ALU.is_lt))
        yield
        p.op("vector", lambda e: e.scalar_tensor_tensor(
            svv[:, :], ssc5[:, :], TWOPI, svv[:, :], ALU.mult, ALU.add),
            inc="VE")
        va = p.c("VE")
        p.op("scalar", lambda e: e.activation(ssin[:, :], svv[:, :], AF.Sin),
             waits=[("VE", va)], inc="AC")
        aa = p.c("AC")
        yield
        p.op("vector", (lambda e, sgn_ap=sgn_ap: e.tensor_scalar(
            ssin[:, :], ssin[:, :], sgn_ap, None, ALU.mult)),
            waits=[("AC", aa)])
        yield
        p.op("vector", lambda e: e.tensor_scalar(svv[:, :], sswp[:, :],
                                                 PHID, PIHALF, ALU.mult,
                                                 ALU.add))
        yield
        p.op("vector", lambda e: e.tensor_scalar_mul(scos[:, :], svv[:, :],
                                                     INV2PI))
        yield
        p.op("vector", lambda e: e.tensor_copy(ssc5.ap().bitcast(I32),
                                               scos[:, :]))
        yield
        p.op("vector", lambda e: e.tensor_copy(scos[:, :],
                                               ssc5.ap().bitcast(I32)))
        yield
        p.op("vector", lambda e: e.scalar_tensor_tensor(
            svv[:, :], scos[:, :], -TWOPI, svv[:, :], ALU.mult, ALU.add))
        yield
        p.op("vector", lambda e: e.tensor_single_scalar(
            ssc5[:, :], svv[:, :], PIF, ALU.is_gt))
        yield
        p.op("vector", lambda e: e.scalar_tensor_tensor(
            svv[:, :], ssc5[:, :], -TWOPI, svv[:, :], ALU.mult, ALU.add))
        yield
        p.op("vector", lambda e: e.tensor_single_scalar(
            ssc5[:, :], svv[:, :], -PIF, ALU.is_lt))
        yield
        p.op("vector", lambda e: e.scalar_tensor_tensor(
            svv[:, :], ssc5[:, :], TWOPI, svv[:, :], ALU.mult, ALU.add),
            inc="VE")
        vb = p.c("VE")
        p.op("scalar", lambda e: e.activation(scos[:, :], svv[:, :], AF.Sin),
             waits=[("VE", vb)], inc="AC")
        ab = p.c("AC")
        yield
        p.op("vector", lambda e: e.engine_nop(), waits=[("AC", ab)],
             inc="VE")
        M[("tdone", hf)] = p.c("VE")

    def vrot_gen(hf):
        hoff = HB * hf
        p.op("vector", (lambda e, hoff=hoff: e.tensor_copy(
            sswp[0:32, :], sbu[32:64, hoff:hoff + HB])),
            waits=[("VE", M[("tdone", hf)]), ("VE", M[("p1end", hf)])])
        yield
        p.op("vector", (lambda e, hoff=hoff: e.tensor_copy(
            sswp[32:64, :], sbu[0:32, hoff:hoff + HB])))
        yield
        p.op("vector", (lambda e, hoff=hoff: e.tensor_mul(
            svv[:, :], sbu[:, hoff:hoff + HB], scos[:, :])))
        yield
        p.op("vector", lambda e: e.tensor_mul(sswp[:, :], sswp[:, :],
                                              ssin[:, :]))
        yield
        p.op("vector", lambda e: e.tensor_add(svv[:, :], svv[:, :],
                                              sswp[:, :]))
        yield
        init = zcol[:, :] if hf == 0 else ucol[:, :]
        p.op("vector", (lambda e, hoff=hoff, init=init: e.tensor_tensor_scan(
            sbu[:, hoff:hoff + HB], RRC.broadcast_to((64, HB)), svv[:, :],
            init, ALU.mult, ALU.add)))
        yield
        if hf == 0:
            p.op("vector", lambda e: e.tensor_copy(ucol[:, :],
                                                   sbu[:, HB - 1:HB]))
            yield
        p.op("vector", lambda e: e.engine_nop(), inc="VE")
        M[("scan", hf)] = p.c("VE")

    t0gen = tables_gen(0, ssgv[:, :], [("VE", M["cstack"])], [])

    def p3pre_gen():
        yield from vrot_gen(0)
        yield from tables_gen(1, ssgv[:, :], [("VE", M[("scan", 0)])], [])

    for c in range(NCH):
        emit_passA_chunk(0, c)
    emit_stats_math(0)
    for c in range(NCH):
        emit_passB_chunk(0, c)
        emit_passA_chunk(1, c)
        next(t0gen, None)
        next(t0gen, None)
    for _ in t0gen:
        pass
    emit_p1end(0)
    emit_stats_math(1)
    pp = p3pre_gen()
    for c in range(NCH):
        emit_passB_chunk(1, c)
        next(pp, None)
        next(pp, None)
        next(pp, None)
    for _ in pp:
        pass
    emit_p1end(1)

    M["p1_done"] = M[("p1end", 1)]
    M["p1_si"] = [("SI0", p.c("SI0")), ("SI1", p.c("SI1"))]
    if debug:
        p.op("gpsimd", lambda e: e.dma_start(out=dbg["dbg_lg"],
                                             in_=lg_sb[:, :]),
             waits=[("VE", M["p1_done"])], inc="SO", amt=16)
        p.op("gpsimd", lambda e: e.dma_start(out=dbg["dbg_xn"],
                                             in_=xn_t[:, :]),
             inc="SO", amt=16)
        p.op("gpsimd", lambda e: e.dma_start(out=dbg["dbg_bu"],
                                             in_=sbu[:, :]),
             inc="SO", amt=16)

    seqmark("p1end")
    t2ev = {}
    for b in range(NCH):
        pt = pb[b % 2]
        w = [("VE", M["p1_done"])]
        if b >= 2:
            w.append(("VE", t2ev[b - 2]))
        p.op("tensor", (lambda e, b=b, pt=pt: e.transpose(
            pt[0:128, 0:8], lg_sb[:, 128 * b:128 * (b + 1)], sid8[:, :])),
            waits=w, inc="PE")
        t = p.c("PE")
        p.op("vector", (lambda e, b=b, pt=pt: e.tensor_copy(
            ltm[:, 8 * b:8 * (b + 1)], pt[0:128, 0:8])),
            waits=[("PE", t)], inc="VE")
        t2ev[b] = p.c("VE")

    def T2(i, n=1):
        return t2s[:, i:i + n]

    L3 = ltm[:, :].rearrange("p (c e) -> p c e", e=8)
    wtw = [("VE", t2ev[NCH - 1])]
    p.op("vector", lambda e: e.tensor_reduce(T2(0, 16), L3, AX.X, ALU.max),
         waits=wtw)
    M1B = t2s[:, 0:16].rearrange("p (c e) -> p c e", e=1).broadcast_to(
        (128, 16, 8))
    p.op("vector", lambda e: e.tensor_tensor(
        t2s[:, 32:160].rearrange("p (c e) -> p c e", e=8), L3, M1B,
        ALU.is_equal))
    p.op("vector", lambda e: e.scalar_tensor_tensor(
        t2s[:, 160:288].rearrange("p (c e) -> p c e", e=8),
        t2s[:, 32:160].rearrange("p (c e) -> p c e", e=8), -1e30, L3,
        ALU.mult, ALU.add))
    p.op("vector", lambda e: e.tensor_reduce(
        T2(16, 16), t2s[:, 160:288].rearrange("p (c e) -> p c e", e=8),
        AX.X, ALU.max))
    p.op("vector", lambda e: e.tensor_sub(T2(288, 16), T2(0, 16),
                                          T2(16, 16)))
    vsig = p.c("VE")
    p.op("scalar", lambda e: e.activation(T2(304, 16), T2(288, 16),
                                          AF.Sigmoid),
         waits=[("VE", vsig)], inc="AC")
    asig = p.c("AC")
    p.op("vector", lambda e: e.tensor_scalar(T2(320, 16), T2(304, 16),
                                             -1.0, 1.0, ALU.mult, ALU.add),
         waits=[("AC", asig)])
    M2B = t2s[:, 16:32].rearrange("p (c e) -> p c e", e=1).broadcast_to(
        (128, 16, 8))
    p.op("vector", lambda e: e.tensor_tensor(
        t2s[:, 336:464].rearrange("p (c e) -> p c e", e=8),
        t2s[:, 160:288].rearrange("p (c e) -> p c e", e=8), M2B,
        ALU.is_equal))
    W1B = t2s[:, 304:320].rearrange("p (c e) -> p c e", e=1).broadcast_to(
        (128, 16, 8))
    W2B = t2s[:, 320:336].rearrange("p (c e) -> p c e", e=1).broadcast_to(
        (128, 16, 8))
    p.op("vector", lambda e: e.tensor_tensor(
        t2s[:, 464:592].rearrange("p (c e) -> p c e", e=8),
        t2s[:, 32:160].rearrange("p (c e) -> p c e", e=8), W1B, ALU.mult))
    p.op("vector", lambda e: e.tensor_tensor(
        t2s[:, 336:464].rearrange("p (c e) -> p c e", e=8),
        t2s[:, 336:464].rearrange("p (c e) -> p c e", e=8), W2B, ALU.mult))
    p.op("vector", lambda e: e.tensor_add(
        t2s[:, 464:592], t2s[:, 464:592], t2s[:, 336:464]))
    vg = p.c("VE")
    gfm_ev = {}
    for b in range(NCH):
        pt = pb[2 + b % 2]
        w = [("VE", vg)]
        if b >= 2:
            w.append(("VE", gfm_ev[b - 2]))
        p.op("tensor", (lambda e, b=b, pt=pt: e.transpose(
            pt[0:8, 0:128], t2s[:, 464 + 8 * b:464 + 8 * (b + 1)],
            sid128[:, :])), waits=w, inc="PE")
        tg = p.c("PE")
        p.op("vector", (lambda e, b=b, pt=pt: e.tensor_copy(
            gfm[:, 128 * b:128 * (b + 1)], pt[0:8, 0:128])),
            waits=[("PE", tg)])
        gfm_ev[b] = p.c("VE")
    M["p2_done"] = gfm_ev[NCH - 1]
    if debug:
        p.op("gpsimd", lambda e: e.dma_start(out=dbg["dbg_gfm"],
                                             in_=gfm[:, :]),
             waits=[("VE", M["p2_done"])], inc="SO", amt=16)

    seqmark("p2end")
    NAPAIR = 3
    wg_last = [0, 0]
    wu_last = [0, 0]
    wd_last = [0, 0, 0]
    gl_wg = {}
    gl_wu = {}
    gl_wd = {}
    au_ev = {}
    bu_ev = {}
    gb_ev = {}
    acst = {}
    gpair = [0]

    WSEM = {"g": "WG", "u": "WU", "d": "WD"}

    def wdma(kind, e_, i, dst_r, src, last, gld):
        nring = 3 if kind == "d" else 2
        idx = e_ * (IKCH if kind != "d" else NCH) + i
        sl = idx % nring
        sem = WSEM[kind] + str(sl)
        w = []
        if idx >= nring:
            w.append(("PE", last[sl]))
        else:
            w.append(("VE", M["p1_done"]))
        width = 2048 if kind != "d" else 1024
        p.op("gpsimd", (lambda e, sl=sl, src=src, dst_r=dst_r, width=width:
                        e.dma_start(out=dst_r[:, width * sl:width * (sl + 1)],
                                    in_=src)), waits=w, inc=sem, amt=16)
        gld[idx] = (sem, p.c(sem))

    def emit_prefetch2(e_):
        for ic in range(2):
            wdma("g", e_, ic, wg_r, dWG[e_, ic], wg_last, gl_wg)
            wdma("u", e_, ic, wu_r, dWU[e_, ic], wu_last, gl_wu)
        for dc in range(2):
            wdma("d", e_, dc, wd_r, dWD[e_, dc], wd_last, gl_wd)

    def emit_gb(e_):
        for tq in range(NTQ):
            w = [("VE", M["p2_done"])]
            if e_ > 0:
                w.append(("VE", au_ev[(e_ - 1, IKCH - 1, NTQ - 1)]))
            p.op("tensor", (lambda e, e_=e_, tq=tq: e.matmul(
                pb[tq][:, :], soh[:, 128 * e_:128 * (e_ + 1)],
                gfm[:, TAU * tq:TAU * (tq + 1)], start=True, stop=True)),
                waits=(w if tq == 0 else []), inc=("PE" if tq == 3 else None))
        M[("gb", e_)] = p.c("PE")
        gbuf = g_sb if e_ % 2 == 0 else g_sbo
        for tq in range(NTQ):
            w = [("PE", M[("gb", e_)])]
            if e_ <= 1 and tq == 0:
                w.extend(M["p1_si"])
            p.op("vector", (lambda e, tq=tq, gbuf=gbuf: e.tensor_copy(
                gbuf[:, TAU * tq:TAU * (tq + 1)], pb[tq][:, :])),
                waits=w)
        gb_ev[e_] = p.c("VE")

    def emit_aic(e_, ic, ig=None):
        sl = (e_ * IKCH + ic) % 2
        wgt, wut, wb = wg_r, wu_r, 2048 * sl
        for tq in range(NTQ):
            u = ic * NTQ + tq
            pair = u % NAPAIR
            bg, bu_ = pb[2 * pair], pb[2 * pair + 1]
            w = [gl_wg[e_ * IKCH + ic], gl_wu[e_ * IKCH + ic],
                 ("VE", gb_ev[e_])]
            if u >= NAPAIR:
                pu = u - NAPAIR
                w.append(("VE", au_ev[(e_, pu // NTQ, pu % NTQ)]))
            elif e_ > 0:
                w.append(("VE", bu_ev[(e_ - 1, NCH - 1, NTQ - 1)]))
            for k in range(NCH):
                p.op("tensor", (lambda e, k=k, tq=tq, wb=wb, wgt=wgt, bg=bg:
                                e.matmul(
                    bg[:, :],
                    wgt[:, wb + 128 * k:wb + 128 * (k + 1)],
                    xn_t[:, TOK * k + TAU * tq:TOK * k + TAU * (tq + 1)],
                    start=(k == 0), stop=(k == NCH - 1))),
                    waits=(w if k == 0 else []))
                p.op("tensor", (lambda e, k=k, tq=tq, wb=wb, wut=wut, bu_=bu_:
                                e.matmul(
                    bu_[:, :],
                    wut[:, wb + 128 * k:wb + 128 * (k + 1)],
                    xn_t[:, TOK * k + TAU * tq:TOK * k + TAU * (tq + 1)],
                    start=(k == 0), stop=(k == NCH - 1))),
                    inc=("PE" if k == NCH - 1 else None))
            pe_u = p.c("PE")
            if tq == NTQ - 1:
                wg_last[sl] = pe_u
                wu_last[sl] = pe_u
            ssl = u % 2
            ws = [("PE", pe_u)]
            if u >= 2:
                pu = u - 2
                ws.append(("VE", au_ev[(e_, pu // NTQ, pu % NTQ)]))
            elif e_ > 0:
                ws.append(("VE", au_ev[(e_ - 1, IKCH - 1, NTQ - 1)]))
            p.op("scalar", (lambda e, ssl=ssl, bg=bg: e.activation(
                s_sc[:, TAU * ssl:TAU * (ssl + 1)], bg[:, :], AF.Silu)),
                waits=ws, inc="AC")
            ac_u = p.c("AC")
            p.op("vector", (lambda e, ssl=ssl, ic=ic, tq=tq, bu_=bu_:
                            e.tensor_tensor(
                h_t[:, TOK * ic + TAU * tq:TOK * ic + TAU * (tq + 1)],
                bu_[:, :],
                s_sc[:, TAU * ssl:TAU * (ssl + 1)], ALU.mult)),
                waits=[("AC", ac_u)], inc="VE")
            au_ev[(e_, ic, tq)] = p.c("VE")
            if ig is not None:
                next(ig, None)
                next(ig, None)
        nic = ic + 2
        if nic < IKCH:
            wdma("g", e_, nic, wg_r, dWG[e_, nic], wg_last, gl_wg)
            wdma("u", e_, nic, wu_r, dWU[e_, nic], wu_last, gl_wu)
        elif e_ + 1 < NE:
            wdma("g", e_ + 1, nic - IKCH, wg_r, dWG[e_ + 1, nic - IKCH],
                 wg_last, gl_wg)
            wdma("u", e_ + 1, nic - IKCH, wu_r, dWU[e_ + 1, nic - IKCH],
                 wu_last, gl_wu)

    emit_prefetch2(0)
    emit_gb(0)

    p.op("vector", lambda e: e.tensor_copy(rowr[:, 0:32], R(6)),
         waits=[("VE", M["p1_done"])])
    p.op("vector", lambda e: e.tensor_copy(rowr[:, 32:64], R(7)), inc="VE")
    vpr = p.c("VE")
    p.op("tensor", lambda e: e.matmul(pb[6][0:8, 0:32], sorr[:, 0:8],
                                      rowr[:, 0:32], start=True, stop=True),
         waits=[("VE", vpr), ("VE", M["p2_done"])])
    p.op("tensor", lambda e: e.matmul(pb7f[0:8, 0:32], sorr[:, 0:8],
                                      rowr[:, 32:64], start=True, stop=True),
         inc="PE")
    M["rp8"] = p.c("PE")

    for _ in vrot_gen(1):
        pass

    if debug:
        p.op("gpsimd", lambda e: e.dma_start(out=dbg["dbg_u"], in_=sbu[:, :]),
             waits=[("VE", M[("scan", 1)])], inc="SO", amt=16)

    seqmark("scans")
    p.op("vector", lambda e: e.tensor_copy(ecol[0:32, 0:1],
                                           sbu[32:64, TOK - 1:TOK]),
         waits=[("VE", M[("scan", 1)])])
    p.op("vector", lambda e: e.tensor_copy(ecol[32:64, 0:1],
                                           sbu[0:32, TOK - 1:TOK]))
    p.op("vector", lambda e: e.tensor_mul(ecol[:, 1:2], sbu[:, TOK - 1:TOK],
                                          scos[:, HB - 1:HB]))
    p.op("vector", lambda e: e.tensor_mul(ecol[:, 2:3], ecol[:, 0:1],
                                          ssin[:, HB - 1:HB]))
    p.op("vector", lambda e: e.tensor_sub(ecol[:, 3:4], ecol[:, 1:2],
                                          ecol[:, 2:3]), inc="VE")
    M["ek"] = p.c("VE")
    p.op("gpsimd", lambda e: e.dma_start(out=dCCI, in_=ecol[:, 3:4],
                                         allow_slow_non_contiguous=True),
         waits=[("VE", M["ek"])], inc="CCL", amt=16)
    gl_e = p.c("CCL")
    seqmark("precc")
    p.op("gpsimd", lambda e: e.collective_compute(
        "AllGather", ALU.bypass, replica_groups=[list(range(8))],
        ins=[dCCI.opt()], outs=[dCCO.opt()]),
        waits=[("CCL", gl_e)], inc="CC", amt=1)
    p.op("gpsimd", lambda e: e.dma_start(out=carr[:, 0:64], in_=dCCO),
         waits=[("CC", 1)], inc="CCL", amt=16)
    M["ccload"] = p.c("CCL")
    if debug:
        p.op("gpsimd", lambda e: e.dma_start(out=dbg["dbg_ek"],
                                             in_=carr[:, 0:64]),
             waits=[("CCL", M["ccload"])], inc="SO", amt=16)

    seqmark("postcc")
    def carry_gen():
        wrp = [("PE", M["rp8"]), ("CCL", M["ccload"]), ("XL", M["p0_sync"])]
        p.op("vector", lambda e: e.tensor_scalar(carr[:, 64:96], pb[6][0:8, 0:32],
                                                 sdst[:, :], None, ALU.mult),
             waits=wrp)
        yield
        p.op("vector", lambda e: e.tensor_scalar(carr[:, 96:128],
                                                 pb7f[0:8, 0:32], sdst[:, :],
                                                 2048.0, ALU.mult, ALU.mult),
             inc="VE")
        yield
        vmag = p.c("VE")
        p.op("scalar", lambda e: e.activation(carr[:, 64:96], carr[:, 64:96],
                                              AF.Exp, scale=2048.0),
             waits=[("VE", vmag)], inc="AC")
        yield
        amag = p.c("AC")
        p.op("vector", lambda e: e.tensor_scalar(carr[:, 64:96], carr[:, 64:96],
                                                 smsk[:, :], None, ALU.mult),
             waits=[("AC", amag)])
        yield
        p.op("vector", lambda e: e.tensor_scalar_mul(carr[:, 128:160],
                                                     carr[:, 96:128], INV2PI))
        yield
        p.op("vector", lambda e: e.tensor_copy(carr[:, 160:192].bitcast(I32),
                                               carr[:, 128:160]))
        yield
        p.op("vector", lambda e: e.tensor_copy(carr[:, 128:160],
                                               carr[:, 160:192].bitcast(I32)))
        yield
        p.op("vector", lambda e: e.scalar_tensor_tensor(
            carr[:, 160:192], carr[:, 128:160], -TWOPI, carr[:, 96:128],
            ALU.mult, ALU.add))
        yield
        p.op("vector", lambda e: e.tensor_single_scalar(
            carr[:, 384:416], carr[:, 160:192], PIF, ALU.is_gt))
        yield
        p.op("vector", lambda e: e.scalar_tensor_tensor(
            carr[:, 160:192], carr[:, 384:416], -TWOPI, carr[:, 160:192],
            ALU.mult, ALU.add))
        yield
        p.op("vector", lambda e: e.tensor_single_scalar(
            carr[:, 384:416], carr[:, 160:192], -PIF, ALU.is_lt))
        yield
        p.op("vector", lambda e: e.scalar_tensor_tensor(
            carr[:, 160:192], carr[:, 384:416], TWOPI, carr[:, 160:192],
            ALU.mult, ALU.add))
        yield
        vth1 = p.c("VE")
        p.op("scalar", lambda e: e.activation(carr[:, 128:160],
                                              carr[:, 160:192], AF.Sin),
             waits=[("VE", vth1)], inc="AC")
        yield
        ath1 = p.c("AC")
        p.op("vector", lambda e: e.tensor_scalar_add(carr[:, 192:224],
                                                     carr[:, 96:128], PIHALF),
             waits=[("AC", ath1)])
        yield
        p.op("vector", lambda e: e.tensor_scalar_mul(carr[:, 224:256],
                                                     carr[:, 192:224], INV2PI))
        yield
        p.op("vector", lambda e: e.tensor_copy(carr[:, 160:192].bitcast(I32),
                                               carr[:, 224:256]))
        yield
        p.op("vector", lambda e: e.tensor_copy(carr[:, 224:256],
                                               carr[:, 160:192].bitcast(I32)))
        yield
        p.op("vector", lambda e: e.scalar_tensor_tensor(
            carr[:, 160:192], carr[:, 224:256], -TWOPI, carr[:, 192:224],
            ALU.mult, ALU.add))
        yield
        p.op("vector", lambda e: e.tensor_single_scalar(
            carr[:, 384:416], carr[:, 160:192], PIF, ALU.is_gt))
        yield
        p.op("vector", lambda e: e.scalar_tensor_tensor(
            carr[:, 160:192], carr[:, 384:416], -TWOPI, carr[:, 160:192],
            ALU.mult, ALU.add))
        yield
        p.op("vector", lambda e: e.tensor_single_scalar(
            carr[:, 384:416], carr[:, 160:192], -PIF, ALU.is_lt))
        yield
        p.op("vector", lambda e: e.scalar_tensor_tensor(
            carr[:, 160:192], carr[:, 384:416], TWOPI, carr[:, 160:192],
            ALU.mult, ALU.add))
        yield
        vth2 = p.c("VE")
        p.op("scalar", lambda e: e.activation(carr[:, 160:192],
                                              carr[:, 160:192], AF.Sin),
             waits=[("VE", vth2)], inc="AC")
        yield
        ath2 = p.c("AC")
        p.op("vector", lambda e: e.tensor_mul(carr[:, 192:224], carr[:, 64:96],
                                              carr[:, 160:192]),
             waits=[("AC", ath2)])
        yield
        p.op("vector", lambda e: e.tensor_mul(carr[:, 224:256], carr[:, 64:96],
                                              carr[:, 128:160]))
        yield
        p.op("vector", lambda e: e.tensor_mul(carr[:, 256:288], carr[:, 192:224],
                                              carr[:, 0:32]))
        yield
        p.op("vector", lambda e: e.tensor_mul(carr[:, 288:320], carr[:, 224:256],
                                              carr[:, 32:64]))
        yield
        p.op("vector", lambda e: e.tensor_sub(carr[:, 256:288], carr[:, 256:288],
                                              carr[:, 288:320]))
        yield
        p.op("vector", lambda e: e.tensor_mul(carr[:, 320:352], carr[:, 192:224],
                                              carr[:, 32:64]))
        yield
        p.op("vector", lambda e: e.tensor_mul(carr[:, 288:320], carr[:, 224:256],
                                              carr[:, 0:32]))
        yield
        p.op("vector", lambda e: e.tensor_add(carr[:, 288:320], carr[:, 288:320],
                                              carr[:, 320:352]), inc="VE")
        yield
        M["pprod"] = p.c("VE")
        p.op("tensor", lambda e: e.matmul(pb[6][0:1, 0:32], so8[:, :],
                                          carr[:, 256:288], start=True,
                                          stop=True),
             waits=[("VE", M["pprod"])])
        yield
        p.op("tensor", lambda e: e.matmul(pb7f[0:1, 0:32], so8[:, :],
                                          carr[:, 288:320], start=True,
                                          stop=True), inc="PE")
        yield
        M["csum"] = p.c("PE")
        p.op("vector", lambda e: e.tensor_copy(prow[:, 352:384],
                                               pb[6][0:1, 0:32]),
             waits=[("PE", M["csum"])])
        yield
        p.op("vector", lambda e: e.tensor_copy(prow[:, 384:416],
                                               pb7f[0:1, 0:32]), inc="VE")
        yield
        M["crows"] = p.c("VE")
        p.op("tensor", lambda e: e.transpose(pb[6][0:32, 0:1],
                                             prow[:, 352:384], sid1[:, :]),
             waits=[("VE", M["crows"])])
        yield
        p.op("tensor", lambda e: e.transpose(pb7f[0:32, 0:1],
                                             prow[:, 384:416], sid1[:, :]),
             inc="PE")
        yield
        M["ctp"] = p.c("PE")
        p.op("vector", lambda e: e.tensor_copy(ecol[0:32, 0:1],
                                               pb[6][0:32, 0:1]),
             waits=[("PE", M["ctp"])])
        yield
        p.op("vector", lambda e: e.tensor_copy(ecol[0:32, 1:2],
                                               pb7f[0:32, 0:1]))
        yield
        p.op("vector", lambda e: e.tensor_mul(ecol[0:32, 2:3], C(6),
                                              ecol[0:32, 0:1]))
        yield
        p.op("vector", lambda e: e.tensor_mul(ecol[0:32, 3:4], C(7),
                                              ecol[0:32, 1:2]))
        yield
        p.op("vector", lambda e: e.tensor_sub(uinc[0:32, 0:1], ecol[0:32, 2:3],
                                              ecol[0:32, 3:4]))
        yield
        p.op("vector", lambda e: e.tensor_mul(ecol[0:32, 2:3], C(6),
                                              ecol[0:32, 1:2]))
        yield
        p.op("vector", lambda e: e.tensor_mul(ecol[0:32, 3:4], C(7),
                                              ecol[0:32, 0:1]))
        yield
        p.op("vector", lambda e: e.tensor_add(ecol[0:32, 2:3], ecol[0:32, 2:3],
                                              ecol[0:32, 3:4]))
        yield
        p.op("vector", lambda e: e.tensor_copy(uinc[32:64, 0:1],
                                               ecol[0:32, 2:3]), inc="VE")
        yield
        M["uinit"] = p.c("VE")

    cg = carry_gen()
    emit_aic(0, 0)
    emit_aic(0, 1)
    emit_aic(0, 2, ig=cg)
    emit_aic(0, 3, ig=cg)

    seqmark("carrydone")
    def xsphase_gen(hf):
        hoff = HB * hf
        yield from tables_gen(hf, ssgx[:, :], [("VE", M["uinit"])])
        tdone = M[("tdone", hf)]
        p.op("vector", lambda e: e.tensor_scalar(ssc5[:, :], sswp[:, :],
                                                 1.0, RHOD, ALU.add,
                                                 ALU.mult),
             waits=[("VE", tdone)], inc="VE")
        vr = p.c("VE")
        p.op("scalar", lambda e: e.activation(sswp[:, :], ssc5[:, :],
                                              AF.Exp),
             waits=[("VE", vr)], inc="AC")
        ar = p.c("AC")
        yield
        p.op("vector", lambda e: e.tensor_scalar(ssc5[:, :], sswp[:, :],
                                                 uinc[:, :], None, ALU.mult),
             waits=[("AC", ar)])
        yield
        p.op("vector", (lambda e, hoff=hoff: e.tensor_add(
            sbu[:, hoff:hoff + HB], sbu[:, hoff:hoff + HB], ssc5[:, :])))
        yield
        p.op("vector", (lambda e, hoff=hoff: e.tensor_copy(
            sswp[0:32, :], sbu[32:64, hoff:hoff + HB])))
        yield
        p.op("vector", (lambda e, hoff=hoff: e.tensor_copy(
            sswp[32:64, :], sbu[0:32, hoff:hoff + HB])))
        yield
        p.op("vector", (lambda e, hoff=hoff: e.tensor_mul(
            svv[:, :], sbu[:, hoff:hoff + HB], scos[:, :])))
        yield
        p.op("vector", lambda e: e.tensor_mul(sswp[:, :], sswp[:, :],
                                              ssin[:, :]))
        yield
        p.op("vector", (lambda e, hoff=hoff: e.tensor_add(
            xs_t[:, hoff:hoff + HB], svv[:, :], sswp[:, :])), inc="VE")
        M[("xs", hf)] = p.c("VE")

    emit_aic(0, 4, ig=cg)
    emit_aic(0, 5, ig=cg)
    emit_aic(0, 6, ig=cg)
    emit_aic(0, 7, ig=cg)
    for _ in cg:
        pass

    if debug:
        p.op("gpsimd", lambda e: e.dma_start(out=dbg["dbg_carr"],
                                             in_=carr[:, :]),
             waits=[("VE", M["uinit"])], inc="SO", amt=16)
        p.op("gpsimd", lambda e: e.dma_start(out=dbg["dbg_uin"],
                                             in_=uinc[:, :],
                                             allow_slow_non_contiguous=True),
             waits=[("VE", M["uinit"])], inc="SO", amt=16)

    def xs_all_gen():
        yield from xsphase_gen(0)
        yield from xsphase_gen(1)

    def emit_p3flush(xsteps):
        for _ in xsteps:
            pass
        M["p3_done"] = M[("xs", 1)]
        if debug:
            p.op("vector", lambda e: e.tensor_copy(sswp[:, :],
                                                   xs_t[:, 0:HB]),
                 waits=[("VE", M["p3_done"])])
            p.op("vector", lambda e: e.tensor_copy(svv[:, :],
                                                   xs_t[:, HB:TOK]),
                 inc="VE")
            vd = p.c("VE")
            p.op("gpsimd", lambda e: e.dma_start(out=dbg["dbg_xs"][:, 0:HB],
                                                 in_=sswp[:, :]),
                 waits=[("VE", vd)], inc="SO", amt=16)
            p.op("gpsimd", lambda e: e.dma_start(out=dbg["dbg_xs"][:, HB:TOK],
                                                 in_=svv[:, :]),
                 inc="SO", amt=16)

    seqmark("p3end")

    y2ev = {}

    def emit_y2():
        for dc in range(NCH):
            for tq in range(NTQ):
                u = dc * NTQ + tq
                bk = pb[6] if u % 2 == 0 else pb7f
                w = [("VE", M["p3_done"]), ("VE", M["cstack"])]
                if u >= 2:
                    w.append(("VE", y2ev[u - 2]))
                else:
                    w.append(("VE", bu_ev[(0, NCH - 1, NTQ - 1)]))
                p.op("tensor", (lambda e, dc=dc, tq=tq, bk=bk: e.matmul(
                    bk[:, :], cstk[:, 128 * dc:128 * (dc + 1)],
                    xs_t[:, TAU * tq:TAU * (tq + 1)], start=True,
                    stop=True)),
                    waits=w, inc="PE")
                pe_y = p.c("PE")
                gp = gpair[0]
                gslot = gp % 4
                ghalf = u % 2
                we = [("PE", pe_y)]
                if gp >= 4 and (gp - 4) in acst:
                    we.append(acst[gp - 4])
                p.op("vector", (lambda e, gslot=gslot, ghalf=ghalf, bk=bk:
                                e.tensor_copy(
                    gsc[:, HB * gslot + TAU * ghalf:
                        HB * gslot + TAU * (ghalf + 1)], bk[:, :])),
                    waits=we, inc="VE")
                y2ev[u] = p.c("VE")
                if ghalf == 1:
                    tqp = tq // 2
                    p.op("gpsimd", (lambda e, dc=dc, tqp=tqp, gslot=gslot:
                                    e.dma_start(
                        out=dOUT[dc, tqp],
                        in_=gsc[:, HB * gslot:HB * (gslot + 1)],
                        accum_op=ALU.add)),
                        waits=[("VE", y2ev[u])] +
                        ([region_so[(dc, tqp)]] if (dc, tqp) in region_so
                         else []),
                        inc="SG%d" % gslot, amt=16)
                    acst[gp] = ("SG%d" % gslot, p.c("SG%d" % gslot))
                    region_so[(dc, tqp)] = acst[gp]
                    gpair[0] += 1

    for e_ in range(NE):
        for ic in range(IKCH if e_ == 0 else 0, IKCH):
            emit_aic(e_, ic)
            if e_ == 1 and ic == 1:
                emit_y2()

        if e_ + 1 < NE:
            emit_gb(e_ + 1)
        gbuf = g_sb if e_ % 2 == 0 else g_sbo
        xsteps = xs_all_gen() if e_ == 0 else None

        for dc in range(NCH):
            ndc = dc + 2
            if ndc < NCH:
                wdma("d", e_, ndc, wd_r, dWD[e_, ndc], wd_last, gl_wd)
            elif e_ + 1 < NE:
                wdma("d", e_ + 1, ndc - NCH, wd_r, dWD[e_ + 1, ndc - NCH],
                     wd_last, gl_wd)
            for tq in range(NTQ):
                u = dc * NTQ + tq
                bk = pb[6] if u % 2 == 0 else pb7f
                w = [gl_wd[e_ * NCH + dc],
                     ("VE", au_ev[(e_, IKCH - 1, NTQ - 1)])]
                if u >= 2:
                    pu = u - 2
                    w.append(("VE", bu_ev[(e_, pu // NTQ, pu % NTQ)]))
                elif e_ == 0:
                    w.append(("VE", M["uinit"]))
                elif e_ == 1:
                    w.append(("VE", y2ev[NCH * NTQ - 1]))
                else:
                    w.append(("VE", bu_ev[(e_ - 1, NCH - 1, NTQ - 1)]))
                sl = (e_ * NCH + dc) % 3
                for ik in range(IKCH):
                    p.op("tensor", (lambda e, ik=ik, tq=tq, sl=sl, bk=bk:
                                    e.matmul(
                        bk[:, :],
                        wd_r[:, 1024 * sl + 128 * ik:1024 * sl + 128 * (ik + 1)],
                        h_t[:, TOK * ik + TAU * tq:TOK * ik + TAU * (tq + 1)],
                        start=(ik == 0), stop=(ik == IKCH - 1))),
                        waits=(w if ik == 0 else []),
                        inc=("PE" if ik == IKCH - 1 else None))
                pe_b = p.c("PE")
                if tq == NTQ - 1:
                    wd_last[sl] = pe_b
                gp = gpair[0]
                gslot = gp % 4
                ghalf = u % 2
                we = [("PE", pe_b)]
                if gp >= 4 and (gp - 4) in acst:
                    we.append(acst[gp - 4])
                p.op("vector", (lambda e, gslot=gslot, ghalf=ghalf, bk=bk,
                                tq=tq, gbuf=gbuf: e.tensor_tensor(
                    gsc[:, HB * gslot + TAU * ghalf:
                        HB * gslot + TAU * (ghalf + 1)], bk[:, :],
                    gbuf[:, TAU * tq:TAU * (tq + 1)], ALU.mult)),
                    waits=we, inc="VE")
                bu_ev[(e_, dc, tq)] = p.c("VE")
                if ghalf == 1:
                    tqp = tq // 2
                    p.op("gpsimd", (lambda e, dc=dc, tqp=tqp, gslot=gslot:
                                    e.dma_start(
                        out=dOUT[dc, tqp],
                        in_=gsc[:, HB * gslot:HB * (gslot + 1)],
                        accum_op=ALU.add)),
                        waits=[("VE", bu_ev[(e_, dc, tq)])] +
                        ([region_so[(dc, tqp)]] if (dc, tqp) in region_so
                         else []),
                        inc="SG%d" % gslot, amt=16)
                    acst[gp] = ("SG%d" % gslot, p.c("SG%d" % gslot))
                    region_so[(dc, tqp)] = acst[gp]
                    gpair[0] += 1
                if xsteps is not None:
                    next(xsteps, None)

        if e_ == 0:
            emit_p3flush(xsteps)

        if debug and e_ == NE - 1:
            p.op("gpsimd", lambda e: e.dma_start(out=dbg["dbg_h"],
                                                 in_=h_t[:, :]),
                 waits=[("VE", au_ev[(NE - 1, IKCH - 1, NTQ - 1)])],
                 inc="SO", amt=16)

    final_waits = [(s, p.c(s)) for s in
                   ("SG0", "SG1", "SG2", "SG3", "SI0", "SI1")
                   if p.c(s) > 0]
    if p.c("SO") > 0:
        final_waits.append(("SO", p.c("SO")))
    p.op("gpsimd", lambda e: e.engine_nop(), waits=final_waits)

    stack = contextlib.ExitStack()
    sems = {}
    for s in ("XL", "XC", "XI", "XA0", "XA1", "WG0", "WG1", "WU0", "WU1", "WD0", "WD1", "WD2", "SG0", "SG1", "SG2", "SG3", "SI0", "SI1", "XR0", "XR1", "CCL", "GL", "SO", "CC", "PE", "VE", "AC", "EW"):
        sems[s] = stack.enter_context(nc.semaphore(s))

    import os
    trunc = int(os.environ.get("KERNEL_TRUNC", "0")) or None
    maxcnt = {}
    if trunc is not None:
        for nm in ENGS:
            for waits, fn, inc, amt, seq in p.q[nm]:
                if seq < trunc and inc is not None:
                    maxcnt[inc] = maxcnt.get(inc, 0) + amt

    def replay(name):
        def go(eng):
            hw = {}
            for waits, fn, inc, amt, seq in p.q[name]:
                if trunc is not None and seq >= trunc:
                    continue
                for s, v in waits:
                    if trunc is not None:
                        v = min(v, maxcnt.get(s, 0))
                    if v > 0 and hw.get(s, -1) < v:
                        eng.wait_ge(sems[s], v)
                        hw[s] = v
                inst = fn(eng)
                if inc is not None:
                    inst.then_inc(sems[inc], amt)
        return go

    with stack:
        with nc.Block() as block:
            block.sync(replay("sync"))
            block.gpsimd(replay("gpsimd"))
            block.vector(replay("vector"))
            block.scalar(replay("scalar"))
            block.tensor(replay("tensor"))
    return nc


def _layout(u, lre, lim, bre, bim, cre, cim, dvec, logd, rg, wgu, wdn):
    f32 = np.float32
    toks = np.ascontiguousarray(u.reshape(16384, 2048))
    xs = [np.ascontiguousarray(
        toks[2048 * k:2048 * (k + 1)].T.reshape(16, 128, 2, 1024)
        .transpose(0, 2, 1, 3)) for k in range(8)]

    gate = wgu[:, :, 0::2]
    up = wgu[:, :, 1::2]
    wg_b = np.ascontiguousarray(
        gate.reshape(8, 16, 128, 8, 128).transpose(0, 3, 2, 1, 4))
    wu_b = np.ascontiguousarray(
        up.reshape(8, 16, 128, 8, 128).transpose(0, 3, 2, 1, 4))
    wd_b = np.ascontiguousarray(
        wdn.reshape(8, 8, 128, 16, 128).transpose(0, 3, 2, 1, 4))
    rgt = np.ascontiguousarray(
        rg.T.reshape(16, 128, 8).transpose(1, 0, 2).reshape(128, 128))

    prm = np.zeros((1, 96), f32)
    prm[0, 0:32] = lre
    prm[0, 32:64] = lim
    prm[0, 64:96] = logd[:32]
    dv = np.ascontiguousarray(dvec.reshape(16, 128).T)
    iota = np.ascontiguousarray(
        np.tile(np.arange(2048, dtype=f32), (64, 1))
        .reshape(64, 2, 1024).transpose(1, 0, 2))
    oh = np.zeros((8, 8 * 128), f32)
    for e in range(8):
        oh[e, 128 * e:128 * (e + 1)] = 1.0
    sgnv = np.concatenate([np.ones((32, 1), f32), -np.ones((32, 1), f32)])
    cvec = np.zeros((128, 8), f32)
    cvec[:, 0] = 1e-5
    cvec[:, 1] = 1.0
    cvec[:, 2] = float(np.pi / 2.0)

    base = {
        "wg": wg_b, "wu": wu_b, "wd": wd_b, "rgt": rgt,
        "bre": np.ascontiguousarray(bre),
        "bim": np.ascontiguousarray(bim),
        "cret": np.ascontiguousarray(cre.T),
        "cimt": np.ascontiguousarray(cim.T),
        "prm": prm, "dvec": dv, "iota64": iota,
        "id128": np.eye(128, dtype=f32), "id8": np.eye(8, dtype=f32),
        "id1": np.ones((1, 1), f32),
        "id32b": np.concatenate([np.eye(32), np.eye(32)], 0).astype(ml_dtypes.bfloat16),
        "onesr_col": np.ones((128, 1), f32),
        "onesr_row": np.ones((1, 128), f32),
        "ones8": np.ones((8, 1), f32),
        "oh": oh, "sgnv": sgnv, "sgnx": np.ascontiguousarray(-sgnv),
        "cvec": cvec,
    }
    in_maps = []
    for k in range(8):
        m = dict(base)
        m["x_fm"] = xs[k]
        dist = np.zeros((8, 1), f32)
        mask = np.zeros((8, 1), f32)
        seq = k // 4
        for kp in range(8):
            if kp // 4 == seq and kp < k:
                mask[kp, 0] = 1.0
                dist[kp, 0] = float(k - 1 - kp)
        m["dist8"] = dist
        m["mask8"] = mask
        in_maps.append(m)
    return in_maps


_NC_CACHE = {}


def kernel(L0_attn_u, L0_attn_Lambda_unconstrained_re, L0_attn_Lambda_im,
           L0_attn_B_re, L0_attn_B_im, L0_attn_C_re, L0_attn_C_im, L0_attn_D,
           L0_attn_log_Delta, L0_ffn_router_gate, L0_ffn_expert_gate_up,
           L0_ffn_expert_down):
    in_maps = _layout(np.asarray(L0_attn_u, np.float32),
                      np.asarray(L0_attn_Lambda_unconstrained_re, np.float32),
                      np.asarray(L0_attn_Lambda_im, np.float32),
                      np.asarray(L0_attn_B_re, np.float32),
                      np.asarray(L0_attn_B_im, np.float32),
                      np.asarray(L0_attn_C_re, np.float32),
                      np.asarray(L0_attn_C_im, np.float32),
                      np.asarray(L0_attn_D, np.float32),
                      np.asarray(L0_attn_log_Delta, np.float32),
                      np.asarray(L0_ffn_router_gate, np.float32),
                      np.asarray(L0_ffn_expert_gate_up, np.float32),
                      np.asarray(L0_ffn_expert_down, np.float32))
    if "nc" not in _NC_CACHE:
        _NC_CACHE["nc"] = build_nc(debug=False)
    nc = _NC_CACHE["nc"]
    res = run_bass_kernel_spmd(nc, in_maps, core_ids=list(range(8)))
    shards = [np.asarray(r["out_fm"]).transpose(0, 2, 1, 3)
              .reshape(2048, 2048).T for r in res.results]
    out = np.concatenate(shards, axis=0).reshape(2, 8192, 2048)
    return np.ascontiguousarray(out, dtype=np.float32)


# version 28 (final)
# speedup vs baseline: 1.0284x; 1.0284x over previous
_LASTP = None
_LASTM = None
"""Fused single-launch megakernel for mk_prb_d1 (LN -> SSM + MoE, residual).

out = x + SSM(LN(x)) + MoE(LN(x)),  x: (2, 8192, 2048) fp32.

Sharding: 16384 tokens -> 8 contiguous 2048-token blocks (each inside one
sequence).  Only the SSM scan carry crosses cores (AllGather of 8x64 f32).

On-chip layout is feature-major ([d on partitions, tokens on free]); the host
pre-transposes x and pre-blocks weights (layout only) so every DMA is
contiguous; the device writes the output feature-major and the host
transposes it back.

All matmuls use bf16 operands (cast during DMA / on op output) with fp32 PSUM
accumulation; elementwise math is fp32; LN statistics use float32r matmuls.
MoE is computed densely per expert with per-token gates
g_e(t) = w1*[l==m1] + w2*[l==m2], w1 = sigmoid(m1-m2) (exactly the
reference's masked math).  The SSM diagonal scan uses the rotation trick:
u_l = e^{-i phi l} s_l gives a real per-channel-decay scan, executed with
tensor_tensor_scan.  Output accumulation (init residual+D-term, y2, expert
partials) uses DMA accumulate-add into HBM.
"""

import contextlib
import numpy as np
import ml_dtypes
import concourse.bass as bass
import concourse.mybir as mybir
from concourse.bass_utils import run_bass_kernel_spmd

F32 = mybir.dt.float32
F32R = mybir.dt.float32r
BF16 = mybir.dt.bfloat16
I32 = mybir.dt.int32
U8 = mybir.dt.uint8
AF = mybir.ActivationFunctionType
ALU = mybir.AluOpType
AX = mybir.AxisListType

D = 2048
TOK = 2048
NCH = 16
IKCH = 8
NE = 8
HN = 32
TAU = 512
NTQ = 4
HB = 1024
TWOPI = float(2.0 * np.pi)
INV2PI = float(1.0 / (2.0 * np.pi))
PIHALF = float(np.pi / 2.0)
PIF = float(np.pi)

ENGS = ("sync", "gpsimd", "vector", "scalar", "tensor")


class Prog:
    def __init__(self):
        self.q = {e: [] for e in ENGS}
        self.cnt = {}

    def sem(self, name):
        self.cnt[name] = 0

    AUTOSEM = {"vector": "VE", "scalar": "AC"}

    def op(self, eng, fn, waits=(), inc="auto", amt=None, ser=True):
        if inc == "auto":
            inc = self.AUTOSEM.get(eng)
        if amt is None:
            amt = 1
        waits = list(waits)
        if ser and eng in self.AUTOSEM:
            s = self.AUTOSEM[eng]
            if self.cnt[s] > 0:
                waits.append((s, self.cnt[s]))
        seq = sum(len(v) for v in self.q.values())
        self.q[eng].append((tuple(waits), fn, inc, amt, seq))
        if inc is not None:
            self.cnt[inc] += amt
            return self.cnt[inc]
        return None

    def c(self, sem):
        return self.cnt[sem]


def build_nc(debug=False):
    nc = bass.Bass("TRN2", target_bir_lowering=False, debug=False)

    dX = nc.dram_tensor("x_fm", [NCH, 2, 128, HB], F32,
                        kind="ExternalInput").ap()
    dWG = nc.dram_tensor("wg", [NE, IKCH, 128, NCH, 128], F32,
                         kind="ExternalInput").ap()
    dWU = nc.dram_tensor("wu", [NE, IKCH, 128, NCH, 128], F32,
                         kind="ExternalInput").ap()
    dWD = nc.dram_tensor("wd", [NE, NCH, 128, IKCH, 128], F32,
                         kind="ExternalInput").ap()
    dRG = nc.dram_tensor("rgt", [128, 128], F32, kind="ExternalInput").ap()
    dBRE = nc.dram_tensor("bre", [HN, D], F32, kind="ExternalInput").ap()
    dBIM = nc.dram_tensor("bim", [HN, D], F32, kind="ExternalInput").ap()
    dCRE = nc.dram_tensor("cret", [HN, D], F32, kind="ExternalInput").ap()
    dCIM = nc.dram_tensor("cimt", [HN, D], F32, kind="ExternalInput").ap()
    dPRM = nc.dram_tensor("prm", [1, 96], F32, kind="ExternalInput").ap()
    dDV = nc.dram_tensor("dvec", [128, 16], F32, kind="ExternalInput").ap()
    dIOTA = nc.dram_tensor("iota64", [2, 64, HB], F32,
                           kind="ExternalInput").ap()
    dID128 = nc.dram_tensor("id128", [128, 128], F32, kind="ExternalInput").ap()
    dID8 = nc.dram_tensor("id8", [8, 8], F32, kind="ExternalInput").ap()
    dID1 = nc.dram_tensor("id1", [1, 1], F32, kind="ExternalInput").ap()
    dID32B = nc.dram_tensor("id32b", [64, 32], BF16, kind="ExternalInput").ap()
    dONESRC = nc.dram_tensor("onesr_col", [128, 1], F32R,
                             kind="ExternalInput").ap()
    dONESRR = nc.dram_tensor("onesr_row", [1, 128], F32R,
                             kind="ExternalInput").ap()
    dONES8 = nc.dram_tensor("ones8", [8, 1], F32, kind="ExternalInput").ap()
    dOH = nc.dram_tensor("oh", [8, NE * 128], F32R, kind="ExternalInput").ap()
    dSGNV = nc.dram_tensor("sgnv", [64, 1], F32, kind="ExternalInput").ap()
    dSGNX = nc.dram_tensor("sgnx", [64, 1], F32, kind="ExternalInput").ap()
    dCVEC = nc.dram_tensor("cvec", [128, 8], F32, kind="ExternalInput").ap()
    dDIST = nc.dram_tensor("dist8", [8, 1], F32, kind="ExternalInput").ap()
    dMASK = nc.dram_tensor("mask8", [8, 1], F32, kind="ExternalInput").ap()

    dOUT = nc.dram_tensor("out_fm", [NCH, 2, 128, HB], F32,
                          kind="ExternalOutput").ap()
    dCCI = nc.dram_tensor("cc_in", [1, 64], F32).ap()
    dCCO = nc.dram_tensor("cc_out", [8, 64], F32).ap()

    dbg = {}
    if debug:
        for nm, shp, dt in [("dbg_bu", [64, TOK], F32),
                            ("dbg_u", [64, TOK], F32),
                            ("dbg_xs", [64, TOK], F32),
                            ("dbg_lg", [8, TOK], F32),
                            ("dbg_gfm", [8, TOK], F32),
                            ("dbg_mr", [1, TOK], F32),
                            ("dbg_rr", [1, TOK], F32),
                            ("dbg_xn", [128, NCH * TOK], BF16),
                            ("dbg_h", [128, IKCH * TOK], BF16),
                            ("dbg_ek", [8, 64], F32),
                            ("dbg_uin", [64, 1], F32),
                            ("dbg_pcol", [64, 24], F32),
                            ("dbg_prow", [1, 512], F32),
                            ("dbg_bst", [128, 1024], BF16),
                            ("dbg_carr", [8, 512], F32)]:
            dbg[nm] = nc.dram_tensor(nm, shp, dt, kind="ExternalOutput").ap()

    xn_t = nc.alloc_sbuf_tensor("xn_t", [128, NCH * TOK], BF16)
    xs_t = nc.alloc_sbuf_tensor("xs_t", [64, TOK], BF16)
    cstk = nc.alloc_sbuf_tensor("cstk", [64, D], BF16)
    bstk = nc.alloc_sbuf_tensor("bstk", [128, NCH * 64], BF16)
    gfm = nc.alloc_sbuf_tensor("gfm", [8, TOK], F32R)
    carr = nc.alloc_sbuf_tensor("carr", [8, 512], F32)
    pcol = nc.alloc_sbuf_tensor("pcol", [64, 24], F32)
    prow = nc.alloc_sbuf_tensor("prow", [1, 512], F32)
    srg = nc.alloc_sbuf_tensor("srg", [128, 128], BF16)
    sid128 = nc.alloc_sbuf_tensor("sid128", [128, 128], F32)
    sid8 = nc.alloc_sbuf_tensor("sid8", [8, 8], F32)
    sid1 = nc.alloc_sbuf_tensor("sid1", [1, 1], F32)
    sid32b = nc.alloc_sbuf_tensor("sid32b", [64, 32], BF16)
    sorc = nc.alloc_sbuf_tensor("sorc", [128, 1], F32R)
    sorr = nc.alloc_sbuf_tensor("sorr", [1, 128], F32R)
    so8 = nc.alloc_sbuf_tensor("so8", [8, 1], F32)
    soh = nc.alloc_sbuf_tensor("soh", [8, NE * 128], F32R)
    ssgv = nc.alloc_sbuf_tensor("ssgv", [64, 1], F32)
    ssgx = nc.alloc_sbuf_tensor("ssgx", [64, 1], F32)
    scv = nc.alloc_sbuf_tensor("scv", [128, 8], F32)
    sdv = nc.alloc_sbuf_tensor("sdv", [128, 16], F32)
    sdst = nc.alloc_sbuf_tensor("sdst", [8, 1], F32)
    smsk = nc.alloc_sbuf_tensor("smsk", [8, 1], F32)
    sprm = nc.alloc_sbuf_tensor("sprm", [1, 96], F32)
    zcol = nc.alloc_sbuf_tensor("zcol", [64, 1], F32)
    ucol = nc.alloc_sbuf_tensor("ucol", [64, 1], F32)
    ecol = nc.alloc_sbuf_tensor("ecol", [64, 4], F32)
    uinc = nc.alloc_sbuf_tensor("uinc", [64, 1], F32)

    OVR_KB = 114
    ovr = nc.alloc_sbuf_tensor("OVR", [128, OVR_KB * 1024], U8)
    base = nc.lookup_mloc(ovr).addr

    def at(name, shape, dt, off_kb):
        return nc.alloc_sbuf_tensor_at(name, shape, dt,
                                       offset=base + int(off_kb * 1024))

    x_sc = at("x_sc", [128, 2 * HB], F32, 0)
    xr_sc = at("xr_sc", [128, 2 * HB], F32R, 8)
    xq_sc = at("xq_sc", [128, 2 * HB], F32R, 16)
    xn32_sc = at("xn32_sc", [128, 2 * HB], F32, 24)
    oi_sc = at("oi_sc", [128, 2 * HB], F32, 32)
    rows = at("rows", [1, 4096], F32, 40)
    rowr = at("rowr", [1, 2048], F32R, 56)
    lg_sb = at("lg_sb", [8, TOK], F32, 64)
    ltm = at("ltm", [128, 128], F32, 72)
    t2s = at("t2s", [128, 640], F32, 72.5)
    wg_r = at("wg_r", [128, 2 * NCH * 128], BF16, 0)
    wu_r = at("wu_r", [128, 2 * NCH * 128], BF16, 8)
    wd_r = at("wd_r", [128, 3 * IKCH * 128], BF16, 16)
    s_sc = at("s_sc", [128, 2 * TAU], F32, 22)
    m_sb = at("m_sb", [128, HB], F32, 75)
    r_sb = at("r_sb", [128, HB], F32, 79)
    g_sb = at("g_sb", [128, TOK], BF16, 30)
    g_sbo = at("g_sbo", [128, TOK], BF16, 34)
    gsc = at("gsc", [128, 4 * HB], F32, 38)
    h_t = at("h_t", [128, IKCH * TOK], BF16, 54)
    sbu = at("sbu", [64, TOK], F32, 86)
    sswp = at("sswp", [64, HB], F32, 94)
    svv = at("svv", [64, HB], F32, 98)
    scos = at("scos", [64, HB], F32, 102)
    ssin = at("ssin", [64, HB], F32, 106)
    ssc5 = at("ssc5", [64, HB], F32, 110)

    pb = [nc.alloc_psum_tensor("pb%d" % i, [128, 512], F32) for i in range(7)]
    pb7b = nc.alloc_psum_tensor("pb7b", [128, 1024], BF16)
    pb7f = pb7b.ap().bitcast(F32)

    p = Prog()
    global _LASTP, _LASTM
    _LASTP = p
    for s in ("XL", "XC", "XI", "XA0", "XA1", "WG0", "WG1", "WU0", "WU1", "WD0", "WD1", "WD2", "SG0", "SG1", "SG2", "SG3", "SI0", "SI1", "XR0", "XR1", "CCL", "GL", "SO", "CC", "PE", "VE", "AC", "EW"):
        p.sem(s)
    M = {}
    _LASTM = M

    def seqmark(tag):
        M[("seq", tag)] = sum(len(v) for v in p.q.values())

    def dma(eng, dst, src, inc, waits=(), slow=False, accum=False):
        if accum:
            fn = (lambda e, a=dst, b=src: e.dma_start(out=a, in_=b,
                                                      accum_op=ALU.add))
        else:
            fn = (lambda e, a=dst, b=src, s=slow:
                  e.dma_start(out=a, in_=b, allow_slow_non_contiguous=s))
        return p.op(eng, fn, waits=waits, inc=inc, amt=16)

    for dst, src, slow in [
            (sprm[:, :], dPRM, False), (scv[:, :], dCVEC, False),
            (sdv[:, :], dDV, False), (sid128[:, :], dID128, False),
            (sid8[:, :], dID8, False), (sid1[:, :], dID1, False),
            (sid32b[:, :], dID32B, False),
            (sorc[:, :], dONESRC, False), (sorr[:, :], dONESRR, False),
            (so8[:, :], dONES8, False), (ssgv[:, :], dSGNV, False),
            (ssgx[:, :], dSGNX, False), (sdst[:, :], dDIST, False),
            (smsk[:, :], dMASK, False)]:
        dma("sync", dst, src, "XL", slow=slow)
    for dst, src in [(sbu[0:HN, 0:D], dBRE),
                     (ssc5[0:HN, :], dBIM[:, 0:HB]),
                     (ssc5[HN:64, :], dBIM[:, HB:D])]:
        dma("scalar", dst, src, "XL")
    M["p0_sync"] = p.c("XL")
    dma("gpsimd", srg[:, :], dRG, "GL")
    dma("gpsimd", soh[:, :], dOH, "GL")
    M["p0_gl"] = p.c("GL")

    def R(i):
        return prow[:, 32 * i:32 * (i + 1)]

    lam_u = sprm[:, 0:32]
    lam_im = sprm[:, 32:64]
    logd = sprm[:, 64:96]
    one_ap = scv[0:1, 1:2]
    wXL = [("XL", M["p0_sync"])]

    p.op("scalar", lambda e: e.activation(R(0), lam_u, AF.Abs), waits=wXL)
    p.op("scalar", lambda e: e.activation(R(1), R(0), AF.Exp, scale=-1.0))
    p.op("scalar", lambda e: e.activation(R(2), R(1), AF.Ln, bias=one_ap))
    p.op("scalar", lambda e: e.activation(R(5), logd, AF.Exp), inc="AC")
    M["sp0"] = p.c("AC")
    p.op("vector", lambda e: e.tensor_scalar_max(R(3), lam_u, 0.0),
         waits=wXL + [("AC", M["sp0"])])
    p.op("vector", lambda e: e.tensor_add(R(0), R(3), R(2)))
    p.op("vector", lambda e: e.tensor_scalar(R(4), R(0), -1.0, -0.0101,
                                             ALU.mult, ALU.add))
    p.op("vector", lambda e: e.tensor_mul(R(6), R(4), R(5)))
    p.op("vector", lambda e: e.tensor_mul(R(7), lam_im, R(5)))
    p.op("vector", lambda e: e.tensor_mul(R(9), R(4), R(4)))
    p.op("vector", lambda e: e.tensor_mul(R(10), lam_im, lam_im))
    p.op("vector", lambda e: e.tensor_add(R(9), R(9), R(10)))
    p.op("vector", lambda e: e.reciprocal(R(10), R(9)), inc="VE")
    M["pch_v"] = p.c("VE")
    p.op("scalar", lambda e: e.activation(R(8), R(6), AF.Exp),
         waits=[("VE", M["pch_v"])], inc="AC")
    M["pch_a"] = p.c("AC")

    def C(i, n=1, p0=0, np_=32):
        return pcol[p0:p0 + np_, i:i + n]

    prev_ev = [0]

    def row2col(row_ap, col_i, extra=()):
        w = [("VE", prev_ev[0])] if prev_ev[0] else []
        p.op("tensor", (lambda e, a=row_ap: e.transpose(
            pb[4][0:32, 0:1], a, sid1[:, :])),
            waits=list(extra) + w, inc="PE")
        t = p.c("PE")
        p.op("vector", (lambda e, ci=col_i: e.tensor_copy(
            pcol[0:32, ci:ci + 1], pb[4][0:32, 0:1])),
            waits=[("PE", t)], inc="VE")
        prev_ev[0] = p.c("VE")

    wp = [("VE", M["pch_v"]), ("AC", M["pch_a"]), ("XL", M["p0_sync"])]
    row2col(R(6), 0, wp)
    row2col(R(7), 1)
    row2col(R(8), 2)
    row2col(R(4), 3)
    row2col(lam_im, 4)
    row2col(R(10), 5)
    M["pcols"] = prev_ev[0]

    p.op("vector", lambda e: e.tensor_scalar_mul(R(12), R(7), INV2PI),
         waits=[("VE", M["pcols"])])
    p.op("vector", lambda e: e.tensor_copy(R(13).bitcast(I32), R(12)))
    p.op("vector", lambda e: e.tensor_copy(R(12), R(13).bitcast(I32)))
    p.op("vector", lambda e: e.scalar_tensor_tensor(
        R(13), R(12), -TWOPI, R(7), ALU.mult, ALU.add))
    p.op("vector", lambda e: e.tensor_single_scalar(R(11), R(13), PIF,
                                                    ALU.is_gt))
    p.op("vector", lambda e: e.scalar_tensor_tensor(
        R(13), R(11), -TWOPI, R(13), ALU.mult, ALU.add))
    p.op("vector", lambda e: e.tensor_single_scalar(R(11), R(13), -PIF,
                                                    ALU.is_lt))
    p.op("vector", lambda e: e.scalar_tensor_tensor(
        R(13), R(11), TWOPI, R(13), ALU.mult, ALU.add))
    v1 = p.c("VE")
    p.op("scalar", lambda e: e.activation(R(14), R(13), AF.Sin),
         waits=[("VE", v1)])
    p.op("vector", lambda e: e.tensor_scalar_add(R(12), R(7), PIHALF),
         waits=[("AC", p.c("AC"))])
    p.op("vector", lambda e: e.tensor_scalar_mul(R(11), R(12), INV2PI))
    p.op("vector", lambda e: e.tensor_copy(R(13).bitcast(I32), R(11)))
    p.op("vector", lambda e: e.tensor_copy(R(11), R(13).bitcast(I32)))
    p.op("vector", lambda e: e.scalar_tensor_tensor(
        R(13), R(11), -TWOPI, R(12), ALU.mult, ALU.add))
    p.op("vector", lambda e: e.tensor_single_scalar(R(12), R(13), PIF,
                                                    ALU.is_gt))
    p.op("vector", lambda e: e.scalar_tensor_tensor(
        R(13), R(12), -TWOPI, R(13), ALU.mult, ALU.add))
    p.op("vector", lambda e: e.tensor_single_scalar(R(12), R(13), -PIF,
                                                    ALU.is_lt))
    p.op("vector", lambda e: e.scalar_tensor_tensor(
        R(13), R(12), TWOPI, R(13), ALU.mult, ALU.add))
    v2 = p.c("VE")
    p.op("scalar", lambda e: e.activation(R(15), R(13), AF.Sin),
         waits=[("VE", v2)])
    M["cossin"] = p.c("AC")
    row2col(R(14), 7, [("AC", M["cossin"])])
    row2col(R(15), 6, [("AC", M["cossin"])])
    M["cossin_v"] = prev_ev[0]

    wcs = [("VE", M["cossin_v"])]
    p.op("vector", lambda e: e.tensor_mul(C(12), C(2), C(6)), waits=wcs)
    p.op("vector", lambda e: e.tensor_scalar_add(C(12), C(12), -1.0))
    p.op("vector", lambda e: e.tensor_mul(C(13), C(2), C(7)))
    p.op("vector", lambda e: e.tensor_mul(C(14), C(12), C(3)))
    p.op("vector", lambda e: e.tensor_mul(C(15), C(13), C(4)))
    p.op("vector", lambda e: e.tensor_add(C(14), C(14), C(15)))
    p.op("vector", lambda e: e.tensor_mul(C(8), C(14), C(5)))
    p.op("vector", lambda e: e.tensor_mul(C(14), C(13), C(3)))
    p.op("vector", lambda e: e.tensor_mul(C(15), C(12), C(4)))
    p.op("vector", lambda e: e.tensor_sub(C(14), C(14), C(15)))
    p.op("vector", lambda e: e.tensor_mul(C(9), C(14), C(5)))
    p.op("vector", lambda e: e.tensor_scalar_mul(C(10), C(9), -1.0))
    p.op("vector", lambda e: e.tensor_copy(pcol[0:32, 16:17], C(2)))
    p.op("vector", lambda e: e.tensor_copy(pcol[32:64, 16:17], C(2)))
    p.op("vector", lambda e: e.tensor_copy(pcol[0:32, 17:18], C(0)))
    p.op("vector", lambda e: e.tensor_copy(pcol[32:64, 17:18], C(0)))
    p.op("vector", lambda e: e.tensor_copy(pcol[0:32, 18:19], C(1)))
    p.op("vector", lambda e: e.tensor_copy(pcol[32:64, 18:19], C(1)))
    p.op("vector", lambda e: e.memset(zcol[:, :], 0.0), inc="VE")
    M["ccoef"] = p.c("VE")
    RRC = pcol[0:64, 16:17]
    RHOD = pcol[0:64, 17:18]
    PHID = pcol[0:64, 18:19]

    def p0tail_gen():
        p.op("vector", lambda e: e.tensor_copy(svv[0:32, :], ssc5[32:64, :]),
             waits=[("XL", M["p0_sync"]), ("VE", M["ccoef"])])
        yield
        p.op("vector", lambda e: e.tensor_scalar(sswp[0:32, :],
                                                 sbu[0:32, 0:HB],
                                                 C(8), None, ALU.mult))
        yield
        p.op("vector", lambda e: e.scalar_tensor_tensor(
            sswp[0:32, :], ssc5[0:32, :], C(10), sswp[0:32, :],
            ALU.mult, ALU.add))
        yield
        p.op("vector", lambda e: e.tensor_scalar(scos[0:32, :],
                                                 ssc5[0:32, :],
                                                 C(8), None, ALU.mult))
        yield
        p.op("vector", lambda e: e.scalar_tensor_tensor(
            scos[0:32, :], sbu[0:32, 0:HB], C(9), scos[0:32, :],
            ALU.mult, ALU.add))
        yield
        p.op("vector", lambda e: e.tensor_scalar(ssin[0:32, :],
                                                 sbu[0:32, HB:D],
                                                 C(8), None, ALU.mult))
        yield
        p.op("vector", lambda e: e.scalar_tensor_tensor(
            ssin[0:32, :], svv[0:32, :], C(10), ssin[0:32, :],
            ALU.mult, ALU.add))
        yield
        p.op("vector", lambda e: e.tensor_scalar(ssc5[0:32, :],
                                                 svv[0:32, :],
                                                 C(8), None, ALU.mult))
        yield
        p.op("vector", lambda e: e.scalar_tensor_tensor(
            ssc5[0:32, :], sbu[0:32, HB:D], C(9), ssc5[0:32, :],
            ALU.mult, ALU.add))
        yield
        p.op("vector", lambda e: e.tensor_copy(xs_t[0:32, 0:HB],
                                               sswp[0:32, :]))
        yield
        p.op("vector", lambda e: e.tensor_copy(xs_t[0:32, HB:D],
                                               ssin[0:32, :]))
        yield
        p.op("vector", lambda e: e.tensor_copy(xs_t[32:64, 0:HB],
                                               scos[0:32, :]))
        yield
        p.op("vector", lambda e: e.tensor_copy(xs_t[32:64, HB:D],
                                               ssc5[0:32, :]), inc="VE")
        M["bbar"] = p.c("VE")
        yield

        dma("scalar", sswp[0:32, :], dCRE[:, 0:HB], "XC",
            waits=[("VE", M["bbar"])])
        dma("scalar", ssin[0:32, :], dCRE[:, HB:D], "XC")
        dma("scalar", svv[0:32, :], dCIM[:, 0:HB], "XC")
        dma("scalar", ssc5[0:32, :], dCIM[:, HB:D], "XC")
        M["cload"] = p.c("XC")
        yield

        prev = M["bbar"]
        for k in range(NCH):
            for half in range(2):
                src = xs_t[0:32, 128 * k:128 * (k + 1)] if half == 0 \
                    else xs_t[32:64, 128 * k:128 * (k + 1)]
                idn = sid32b[0:32, :] if half == 0 else sid32b[32:64, :]
                coff = 64 * k + 32 * half
                p.op("tensor", (lambda e, src=src, idn=idn: e.transpose(
                    pb7b.ap()[0:128, 0:32], src, idn)),
                    waits=[("VE", prev)], inc="PE")
                t = p.c("PE")
                p.op("vector", (lambda e, coff=coff: e.tensor_copy(
                    bstk[:, coff:coff + 32], pb7b.ap()[0:128, 0:32])),
                    waits=[("PE", t)], inc="VE")
                prev = p.c("VE")
                yield
        M["bstack"] = prev
        p.op("vector", lambda e: e.tensor_scalar_mul(cstk[0:32, 0:HB],
                                                     sswp[0:32, :], 2.0),
             waits=[("XC", M["cload"])])
        yield
        p.op("vector", lambda e: e.tensor_scalar_mul(cstk[0:32, HB:D],
                                                     ssin[0:32, :], 2.0))
        yield
        p.op("vector", lambda e: e.tensor_scalar_mul(cstk[32:64, 0:HB],
                                                     svv[0:32, :], -2.0))
        yield
        p.op("vector", lambda e: e.tensor_scalar_mul(cstk[32:64, HB:D],
                                                     ssc5[0:32, :], -2.0),
             inc="VE")
        M["cstack"] = p.c("VE")

    seqmark("p0end")
    s_row = rows[:, 0:1024]
    q_row = rows[:, 1024:2048]
    t_row = rows[:, 2048:3072]
    v_row = rows[:, 3072:4096]
    m_r = rowr[:, 0:1024]
    r_r = rowr[:, 1024:2048]

    region_so = {}
    xr_pe = [0, 0]
    xr_ac = [0, 0]
    xr_ve = [0, 0]
    xsl_ve = [0, 0]
    xn_ac = [0, 0]
    oi_so = [None, None]

    def emit_passA_chunk(hf, c):
        hoff = HB * hf
        sl = c % 2
        so = sl * HB
        if hf == 0:
            wD = []
            if xr_ve[sl]:
                wD.append(("VE", xr_ve[sl]))
            if xsl_ve[sl]:
                wD.append(("VE", xsl_ve[sl]))
            if xr_ac[sl]:
                wD.append(("AC", xr_ac[sl]))
            dma("sync", x_sc[:, so:so + HB],
                dX[c, hf], "XA%d" % sl,
                waits=wD)
            xl = p.c("XA%d" % sl)
            wC = [("XA%d" % sl, xl)]
            if xr_pe[sl]:
                wC.append(("PE", xr_pe[sl]))
            p.op("vector", (lambda e, so=so: e.tensor_copy(
                xr_sc[:, so:so + HB], x_sc[:, so:so + HB])), waits=wC)
            xr_ve[sl] = p.c("VE")
            wS = [("VE", xr_ve[sl])]
            sq_src = x_sc
            wQ = [("XA%d" % sl, xl)]
        else:
            wD = []
            if xr_pe[sl]:
                wD.append(("PE", xr_pe[sl]))
            if xr_ac[sl]:
                wD.append(("AC", xr_ac[sl]))
            dma("gpsimd", xr_sc[:, so:so + HB],
                dX[c, hf], "XR%d" % sl,
                waits=wD)
            wS = [("XR%d" % sl, p.c("XR%d" % sl))]
            sq_src = xr_sc
            wQ = list(wS)
        if xr_pe[sl]:
            wS.append(("PE", xr_pe[sl]))
            wQ.append(("PE", xr_pe[sl]))
        p.op("scalar", (lambda e, so=so, sq_src=sq_src: e.activation(
            xq_sc[:, so:so + HB], sq_src[:, so:so + HB], AF.Square)),
            waits=wQ, inc="AC")
        xr_ac[sl] = p.c("AC")
        wM = list(wS)
        if hf == 1 and c == 0:
            wM.append(("VE", M[("mrrr", 0)]))
        for tq in range(2):
            p.op("tensor", (lambda e, so=so, tq=tq, c=c: e.matmul(
                pb[tq][0:1, :], sorc[:, :],
                xr_sc[:, so + TAU * tq:so + TAU * (tq + 1)],
                start=(c == 0), stop=(c == NCH - 1))),
                waits=(wM if tq == 0 else []))
        for tq in range(2):
            p.op("tensor", (lambda e, so=so, tq=tq, c=c: e.matmul(
                pb[2 + tq][0:1, :], sorc[:, :],
                xq_sc[:, so + TAU * tq:so + TAU * (tq + 1)],
                start=(c == 0), stop=(c == NCH - 1))),
                waits=([("AC", xr_ac[sl])] if tq == 0 else []),
                inc=("PE" if tq == 1 else None))
        xr_pe[sl] = p.c("PE")

    def emit_stats_math(hf):
        M[("statmm_end", hf)] = p.c("PE")
        wst = [("PE", M[("statmm_end", hf)])]
        p.op("vector", lambda e: e.tensor_copy(s_row[:, 0:TAU],
                                               pb[0][0:1, :]), waits=wst)
        p.op("vector", lambda e: e.tensor_copy(s_row[:, TAU:HB],
                                               pb[1][0:1, :]))
        p.op("vector", lambda e: e.tensor_copy(q_row[:, 0:TAU],
                                               pb[2][0:1, :]))
        p.op("vector", lambda e: e.tensor_copy(q_row[:, TAU:HB],
                                               pb[3][0:1, :]))
        p.op("vector", lambda e: e.tensor_scalar_mul(s_row, s_row, 1.0 / D))
        p.op("vector", lambda e: e.tensor_mul(t_row, s_row, s_row))
        p.op("vector", lambda e: e.tensor_scalar_mul(v_row, q_row, 1.0 / D))
        p.op("vector", lambda e: e.tensor_sub(v_row, v_row, t_row))
        vvar = p.c("VE")
        p.op("scalar", lambda e: e.activation(t_row, v_row, AF.Ln,
                                              bias=scv[0:1, 0:1]),
             waits=[("VE", vvar)])
        p.op("scalar", lambda e: e.activation(v_row, t_row, AF.Exp,
                                              scale=-0.5), inc="AC")
        arstd = p.c("AC")
        p.op("vector", lambda e: e.tensor_copy(m_r, s_row),
             waits=[("AC", arstd)])
        p.op("vector", lambda e: e.tensor_copy(r_r, v_row))
        M[("mrrr", hf)] = p.c("VE")
        wbc = [("VE", M[("mrrr", hf)])]
        if hf == 0:
            wbc.append(("VE", M["bstack"]))
        else:
            wbc.append(("VE", M[("p1end", 0)]))
        p.op("tensor", lambda e: e.matmul(pb[4][:, :], sorr[:, :],
                                          m_r[:, 0:TAU], start=True,
                                          stop=True), waits=wbc)
        p.op("tensor", lambda e: e.matmul(pb[5][:, :], sorr[:, :],
                                          m_r[:, TAU:HB], start=True,
                                          stop=True))
        p.op("tensor", lambda e: e.matmul(pb[6][:, :], sorr[:, :],
                                          r_r[:, 0:TAU], start=True,
                                          stop=True))
        p.op("tensor", lambda e: e.matmul(pb7f[:, :], sorr[:, :],
                                          r_r[:, TAU:HB], start=True,
                                          stop=True), inc="PE")
        M[("bcast", hf)] = p.c("PE")
        p.op("vector", lambda e: e.tensor_copy(m_sb[:, 0:TAU], pb[4][:, :]),
             waits=[("PE", M[("bcast", hf)])])
        p.op("vector", lambda e: e.tensor_copy(m_sb[:, TAU:HB], pb[5][:, :]))
        p.op("vector", lambda e: e.tensor_copy(r_sb[:, 0:TAU], pb[6][:, :]))
        p.op("vector", lambda e: e.tensor_copy(r_sb[:, TAU:HB], pb7f[:, :]))
        M[("bcastv", hf)] = p.c("VE")

    def emit_passB_chunk(hf, c):
        hoff = HB * hf
        sl = c % 2
        so = sl * HB
        wD = []
        if xsl_ve[sl]:
            wD.append(("VE", xsl_ve[sl]))
        if hf == 0 and xr_ve[sl]:
            wD.append(("VE", xr_ve[sl]))
        if hf == 0 and xr_ac[sl]:
            wD.append(("AC", xr_ac[sl]))
        dma("sync", x_sc[:, so:so + HB],
            dX[c, hf], "XA%d" % sl,
            waits=wD)
        xl = p.c("XA%d" % sl)
        wv = [("XA%d" % sl, xl), ("VE", M[("bcastv", hf)])]
        if xn_ac[sl]:
            wv.append(("AC", xn_ac[sl]))
        if oi_so[sl]:
            wv.append(oi_so[sl])
        p.op("vector", (lambda e, so=so: e.tensor_tensor(
            xn32_sc[:, so:so + HB], x_sc[:, so:so + HB],
            m_sb[:, :], ALU.subtract)), waits=wv)
        p.op("vector", (lambda e, so=so: e.tensor_tensor(
            xn32_sc[:, so:so + HB], xn32_sc[:, so:so + HB],
            r_sb[:, :], ALU.mult)))
        p.op("vector", (lambda e, so=so, c=c: e.scalar_tensor_tensor(
            oi_sc[:, so:so + HB], xn32_sc[:, so:so + HB],
            sdv[:, c:c + 1], x_sc[:, so:so + HB], ALU.mult, ALU.add)))
        xsl_ve[sl] = p.c("VE")
        M[("xn32", hf, c)] = xsl_ve[sl]
        p.op("scalar", (lambda e, so=so, c=c, hoff=hoff: e.activation(
            xn_t[:, TOK * c + hoff:TOK * c + hoff + HB],
            xn32_sc[:, so:so + HB], AF.Copy)),
            waits=[("VE", xsl_ve[sl])], inc="AC")
        xn_ac[sl] = p.c("AC")
        M[("xnbf", hf, c)] = xn_ac[sl]
        p.op("scalar", (lambda e, so=so, c=c, hf=hf: e.dma_start(
            out=dOUT[c, hf],
            in_=oi_sc[:, so:so + HB])), inc="SI%d" % sl, amt=16)
        oi_so[sl] = ("SI%d" % sl, p.c("SI%d" % sl))
        region_so[(c, hf)] = oi_so[sl]
        wmm = [("AC", xn_ac[sl])]
        if c == 0:
            wmm.append(("GL", M["p0_gl"]))
            wmm.append(("VE", M[("bcastv", hf)]))
        for tq in range(2):
            p.op("tensor", (lambda e, c=c, tq=tq, hoff=hoff: e.matmul(
                pb[4 + tq][0:64, :], bstk[:, 64 * c:64 * (c + 1)],
                xn_t[:, TOK * c + hoff + TAU * tq:
                     TOK * c + hoff + TAU * (tq + 1)],
                start=(c == 0), stop=(c == NCH - 1))),
                waits=(wmm if tq == 0 else []))
            p.op("tensor", (lambda e, c=c, tq=tq, hoff=hoff: e.matmul(
                (pb[6] if tq == 0 else pb7f)[0:8, :],
                srg[:, 8 * c:8 * (c + 1)],
                xn_t[:, TOK * c + hoff + TAU * tq:
                     TOK * c + hoff + TAU * (tq + 1)],
                start=(c == 0), stop=(c == NCH - 1))),
                inc=("PE" if tq == 1 else None))
        M[("bulg", hf, c)] = p.c("PE")

    def emit_p1end(hf):
        hoff = HB * hf
        M[("bulg_end", hf)] = p.c("PE")
        wbu = [("PE", M[("bulg_end", hf)])]
        p.op("vector", (lambda e, hoff=hoff: e.tensor_copy(
            sbu[:, hoff:hoff + TAU], pb[4][0:64, :])), waits=wbu)
        p.op("vector", (lambda e, hoff=hoff: e.tensor_copy(
            sbu[:, hoff + TAU:hoff + HB], pb[5][0:64, :])))
        p.op("vector", (lambda e, hoff=hoff: e.tensor_copy(
            lg_sb[:, hoff:hoff + TAU], pb[6][0:8, :])))
        p.op("vector", (lambda e, hoff=hoff: e.tensor_copy(
            lg_sb[:, hoff + TAU:hoff + HB], pb7f[0:8, :])))
        M[("p1end", hf)] = p.c("VE")

    def tables_gen(hf, sgn_ap, pre, guard=None):
        hoff = HB * hf
        if guard is None:
            guard = [("VE", p.c("VE"))]
        dma("sync", sswp[:, :], dIOTA[hf], "XI",
            waits=list(pre) + list(guard))
        xl = p.c("XI")
        yield
        p.op("vector", lambda e: e.tensor_scalar(svv[:, :], sswp[:, :],
                                                 PHID, None, ALU.mult),
             waits=[("XI", xl)] + list(pre))
        yield
        p.op("vector", lambda e: e.tensor_scalar_mul(ssin[:, :], svv[:, :],
                                                     INV2PI))
        yield
        p.op("vector", lambda e: e.tensor_copy(ssc5.ap().bitcast(I32),
                                               ssin[:, :]))
        yield
        p.op("vector", lambda e: e.tensor_copy(ssin[:, :],
                                               ssc5.ap().bitcast(I32)))
        yield
        p.op("vector", lambda e: e.scalar_tensor_tensor(
            svv[:, :], ssin[:, :], -TWOPI, svv[:, :], ALU.mult, ALU.add))
        yield
        p.op("vector", lambda e: e.tensor_single_scalar(
            ssc5[:, :], svv[:, :], PIF, ALU.is_gt))
        yield
        p.op("vector", lambda e: e.scalar_tensor_tensor(
            svv[:, :], ssc5[:, :], -TWOPI, svv[:, :], ALU.mult, ALU.add))
        yield
        p.op("vector", lambda e: e.tensor_single_scalar(
            ssc5[:, :], svv[:, :], -PIF, ALU.is_lt))
        yield
        p.op("vector", lambda e: e.scalar_tensor_tensor(
            svv[:, :], ssc5[:, :], TWOPI, svv[:, :], ALU.mult, ALU.add),
            inc="VE")
        va = p.c("VE")
        p.op("scalar", lambda e: e.activation(ssin[:, :], svv[:, :], AF.Sin),
             waits=[("VE", va)], inc="AC")
        aa = p.c("AC")
        yield
        p.op("vector", (lambda e, sgn_ap=sgn_ap: e.tensor_scalar(
            ssin[:, :], ssin[:, :], sgn_ap, None, ALU.mult)),
            waits=[("AC", aa)])
        yield
        p.op("vector", lambda e: e.tensor_scalar(svv[:, :], sswp[:, :],
                                                 PHID, PIHALF, ALU.mult,
                                                 ALU.add))
        yield
        p.op("vector", lambda e: e.tensor_scalar_mul(scos[:, :], svv[:, :],
                                                     INV2PI))
        yield
        p.op("vector", lambda e: e.tensor_copy(ssc5.ap().bitcast(I32),
                                               scos[:, :]))
        yield
        p.op("vector", lambda e: e.tensor_copy(scos[:, :],
                                               ssc5.ap().bitcast(I32)))
        yield
        p.op("vector", lambda e: e.scalar_tensor_tensor(
            svv[:, :], scos[:, :], -TWOPI, svv[:, :], ALU.mult, ALU.add))
        yield
        p.op("vector", lambda e: e.tensor_single_scalar(
            ssc5[:, :], svv[:, :], PIF, ALU.is_gt))
        yield
        p.op("vector", lambda e: e.scalar_tensor_tensor(
            svv[:, :], ssc5[:, :], -TWOPI, svv[:, :], ALU.mult, ALU.add))
        yield
        p.op("vector", lambda e: e.tensor_single_scalar(
            ssc5[:, :], svv[:, :], -PIF, ALU.is_lt))
        yield
        p.op("vector", lambda e: e.scalar_tensor_tensor(
            svv[:, :], ssc5[:, :], TWOPI, svv[:, :], ALU.mult, ALU.add),
            inc="VE")
        vb = p.c("VE")
        p.op("scalar", lambda e: e.activation(scos[:, :], svv[:, :], AF.Sin),
             waits=[("VE", vb)], inc="AC")
        ab = p.c("AC")
        yield
        p.op("vector", lambda e: e.engine_nop(), waits=[("AC", ab)],
             inc="VE")
        M[("tdone", hf)] = p.c("VE")

    def vrot_gen(hf):
        hoff = HB * hf
        p.op("vector", (lambda e, hoff=hoff: e.tensor_copy(
            sswp[0:32, :], sbu[32:64, hoff:hoff + HB])),
            waits=[("VE", M[("tdone", hf)]), ("VE", M[("p1end", hf)])])
        yield
        p.op("vector", (lambda e, hoff=hoff: e.tensor_copy(
            sswp[32:64, :], sbu[0:32, hoff:hoff + HB])))
        yield
        p.op("vector", (lambda e, hoff=hoff: e.tensor_mul(
            svv[:, :], sbu[:, hoff:hoff + HB], scos[:, :])))
        yield
        p.op("vector", lambda e: e.tensor_mul(sswp[:, :], sswp[:, :],
                                              ssin[:, :]))
        yield
        p.op("vector", lambda e: e.tensor_add(svv[:, :], svv[:, :],
                                              sswp[:, :]))
        yield
        init = zcol[:, :] if hf == 0 else ucol[:, :]
        p.op("vector", (lambda e, hoff=hoff, init=init: e.tensor_tensor_scan(
            sbu[:, hoff:hoff + HB], RRC.broadcast_to((64, HB)), svv[:, :],
            init, ALU.mult, ALU.add)))
        yield
        if hf == 0:
            p.op("vector", lambda e: e.tensor_copy(ucol[:, :],
                                                   sbu[:, HB - 1:HB]))
            yield
        p.op("vector", lambda e: e.engine_nop(), inc="VE")
        M[("scan", hf)] = p.c("VE")

    def p3pre_gen():
        yield from vrot_gen(0)
        yield from tables_gen(1, ssgv[:, :], [("VE", M[("scan", 0)])], [])

    p0t = p0tail_gen()
    for c in range(NCH):
        emit_passA_chunk(0, c)
        next(p0t, None)
        next(p0t, None)
        next(p0t, None)
        next(p0t, None)
    for _ in p0t:
        pass
    emit_stats_math(0)
    t0gen = tables_gen(0, ssgv[:, :], [("VE", M["cstack"])], [])
    for c in range(NCH):
        emit_passB_chunk(0, c)
        emit_passA_chunk(1, c)
        next(t0gen, None)
        next(t0gen, None)
    for _ in t0gen:
        pass
    emit_p1end(0)
    emit_stats_math(1)
    pp = p3pre_gen()
    for c in range(NCH):
        emit_passB_chunk(1, c)
        next(pp, None)
        next(pp, None)
        next(pp, None)
    for _ in pp:
        pass
    emit_p1end(1)

    M["p1_done"] = M[("p1end", 1)]
    M["p1_si"] = [("SI0", p.c("SI0")), ("SI1", p.c("SI1"))]
    if debug:
        p.op("gpsimd", lambda e: e.dma_start(out=dbg["dbg_lg"],
                                             in_=lg_sb[:, :]),
             waits=[("VE", M["p1_done"])], inc="SO", amt=16)
        p.op("gpsimd", lambda e: e.dma_start(out=dbg["dbg_xn"],
                                             in_=xn_t[:, :]),
             inc="SO", amt=16)
        p.op("gpsimd", lambda e: e.dma_start(out=dbg["dbg_bu"],
                                             in_=sbu[:, :]),
             inc="SO", amt=16)

    seqmark("p1end")
    t2ev = {}
    for b in range(NCH):
        pt = pb[b % 2]
        w = [("VE", M["p1_done"])]
        if b >= 2:
            w.append(("VE", t2ev[b - 2]))
        p.op("tensor", (lambda e, b=b, pt=pt: e.transpose(
            pt[0:128, 0:8], lg_sb[:, 128 * b:128 * (b + 1)], sid8[:, :])),
            waits=w, inc="PE")
        t = p.c("PE")
        p.op("vector", (lambda e, b=b, pt=pt: e.tensor_copy(
            ltm[:, 8 * b:8 * (b + 1)], pt[0:128, 0:8])),
            waits=[("PE", t)], inc="VE")
        t2ev[b] = p.c("VE")

    def T2(i, n=1):
        return t2s[:, i:i + n]

    L3 = ltm[:, :].rearrange("p (c e) -> p c e", e=8)
    wtw = [("VE", t2ev[NCH - 1])]
    p.op("vector", lambda e: e.tensor_reduce(T2(0, 16), L3, AX.X, ALU.max),
         waits=wtw)
    M1B = t2s[:, 0:16].rearrange("p (c e) -> p c e", e=1).broadcast_to(
        (128, 16, 8))
    p.op("vector", lambda e: e.tensor_tensor(
        t2s[:, 32:160].rearrange("p (c e) -> p c e", e=8), L3, M1B,
        ALU.is_equal))
    p.op("vector", lambda e: e.scalar_tensor_tensor(
        t2s[:, 160:288].rearrange("p (c e) -> p c e", e=8),
        t2s[:, 32:160].rearrange("p (c e) -> p c e", e=8), -1e30, L3,
        ALU.mult, ALU.add))
    p.op("vector", lambda e: e.tensor_reduce(
        T2(16, 16), t2s[:, 160:288].rearrange("p (c e) -> p c e", e=8),
        AX.X, ALU.max))
    p.op("vector", lambda e: e.tensor_sub(T2(288, 16), T2(0, 16),
                                          T2(16, 16)))
    vsig = p.c("VE")
    p.op("scalar", lambda e: e.activation(T2(304, 16), T2(288, 16),
                                          AF.Sigmoid),
         waits=[("VE", vsig)], inc="AC")
    asig = p.c("AC")
    p.op("vector", lambda e: e.tensor_scalar(T2(320, 16), T2(304, 16),
                                             -1.0, 1.0, ALU.mult, ALU.add),
         waits=[("AC", asig)])
    M2B = t2s[:, 16:32].rearrange("p (c e) -> p c e", e=1).broadcast_to(
        (128, 16, 8))
    p.op("vector", lambda e: e.tensor_tensor(
        t2s[:, 336:464].rearrange("p (c e) -> p c e", e=8),
        t2s[:, 160:288].rearrange("p (c e) -> p c e", e=8), M2B,
        ALU.is_equal))
    W1B = t2s[:, 304:320].rearrange("p (c e) -> p c e", e=1).broadcast_to(
        (128, 16, 8))
    W2B = t2s[:, 320:336].rearrange("p (c e) -> p c e", e=1).broadcast_to(
        (128, 16, 8))
    p.op("vector", lambda e: e.tensor_tensor(
        t2s[:, 464:592].rearrange("p (c e) -> p c e", e=8),
        t2s[:, 32:160].rearrange("p (c e) -> p c e", e=8), W1B, ALU.mult))
    p.op("vector", lambda e: e.tensor_tensor(
        t2s[:, 336:464].rearrange("p (c e) -> p c e", e=8),
        t2s[:, 336:464].rearrange("p (c e) -> p c e", e=8), W2B, ALU.mult))
    p.op("vector", lambda e: e.tensor_add(
        t2s[:, 464:592], t2s[:, 464:592], t2s[:, 336:464]))
    vg = p.c("VE")
    gfm_ev = {}
    for b in range(NCH):
        pt = pb[2 + b % 2]
        w = [("VE", vg)]
        if b >= 2:
            w.append(("VE", gfm_ev[b - 2]))
        p.op("tensor", (lambda e, b=b, pt=pt: e.transpose(
            pt[0:8, 0:128], t2s[:, 464 + 8 * b:464 + 8 * (b + 1)],
            sid128[:, :])), waits=w, inc="PE")
        tg = p.c("PE")
        p.op("vector", (lambda e, b=b, pt=pt: e.tensor_copy(
            gfm[:, 128 * b:128 * (b + 1)], pt[0:8, 0:128])),
            waits=[("PE", tg)])
        gfm_ev[b] = p.c("VE")
    M["p2_done"] = gfm_ev[NCH - 1]
    if debug:
        p.op("gpsimd", lambda e: e.dma_start(out=dbg["dbg_gfm"],
                                             in_=gfm[:, :]),
             waits=[("VE", M["p2_done"])], inc="SO", amt=16)

    seqmark("p2end")
    NAPAIR = 3
    wg_last = [0, 0]
    wu_last = [0, 0]
    wd_last = [0, 0, 0]
    gl_wg = {}
    gl_wu = {}
    gl_wd = {}
    au_ev = {}
    bu_ev = {}
    gb_ev = {}
    acst = {}
    gpair = [0]

    WSEM = {"g": "WG", "u": "WU", "d": "WD"}

    def wdma(kind, e_, i, dst_r, src, last, gld):
        nring = 3 if kind == "d" else 2
        idx = e_ * (IKCH if kind != "d" else NCH) + i
        sl = idx % nring
        sem = WSEM[kind] + str(sl)
        w = []
        if idx >= nring:
            w.append(("PE", last[sl]))
        else:
            w.append(("VE", M["p1_done"]))
        width = 2048 if kind != "d" else 1024
        p.op("gpsimd", (lambda e, sl=sl, src=src, dst_r=dst_r, width=width:
                        e.dma_start(out=dst_r[:, width * sl:width * (sl + 1)],
                                    in_=src)), waits=w, inc=sem, amt=16)
        gld[idx] = (sem, p.c(sem))

    def emit_prefetch2(e_):
        for ic in range(2):
            wdma("g", e_, ic, wg_r, dWG[e_, ic], wg_last, gl_wg)
            wdma("u", e_, ic, wu_r, dWU[e_, ic], wu_last, gl_wu)
        for dc in range(2):
            wdma("d", e_, dc, wd_r, dWD[e_, dc], wd_last, gl_wd)

    def emit_gb(e_):
        for tq in range(NTQ):
            w = [("VE", M["p2_done"])]
            if e_ > 0:
                w.append(("VE", au_ev[(e_ - 1, IKCH - 1, NTQ - 1)]))
            p.op("tensor", (lambda e, e_=e_, tq=tq: e.matmul(
                pb[tq][:, :], soh[:, 128 * e_:128 * (e_ + 1)],
                gfm[:, TAU * tq:TAU * (tq + 1)], start=True, stop=True)),
                waits=(w if tq == 0 else []), inc=("PE" if tq == 3 else None))
        M[("gb", e_)] = p.c("PE")
        gbuf = g_sb if e_ % 2 == 0 else g_sbo
        for tq in range(NTQ):
            w = [("PE", M[("gb", e_)])]
            if e_ <= 1 and tq == 0:
                w.extend(M["p1_si"])
            p.op("vector", (lambda e, tq=tq, gbuf=gbuf: e.tensor_copy(
                gbuf[:, TAU * tq:TAU * (tq + 1)], pb[tq][:, :])),
                waits=w)
        gb_ev[e_] = p.c("VE")

    def emit_aic(e_, ic, ig=None):
        sl = (e_ * IKCH + ic) % 2
        wgt, wut, wb = wg_r, wu_r, 2048 * sl
        for tq in range(NTQ):
            u = ic * NTQ + tq
            pair = u % NAPAIR
            bg, bu_ = pb[2 * pair], pb[2 * pair + 1]
            w = [gl_wg[e_ * IKCH + ic], gl_wu[e_ * IKCH + ic],
                 ("VE", gb_ev[e_])]
            if u >= NAPAIR:
                pu = u - NAPAIR
                w.append(("VE", au_ev[(e_, pu // NTQ, pu % NTQ)]))
            elif e_ > 0:
                w.append(("VE", bu_ev[(e_ - 1, NCH - 1, NTQ - 1)]))
            for k in range(NCH):
                p.op("tensor", (lambda e, k=k, tq=tq, wb=wb, wgt=wgt, bg=bg:
                                e.matmul(
                    bg[:, :],
                    wgt[:, wb + 128 * k:wb + 128 * (k + 1)],
                    xn_t[:, TOK * k + TAU * tq:TOK * k + TAU * (tq + 1)],
                    start=(k == 0), stop=(k == NCH - 1))),
                    waits=(w if k == 0 else []))
                p.op("tensor", (lambda e, k=k, tq=tq, wb=wb, wut=wut, bu_=bu_:
                                e.matmul(
                    bu_[:, :],
                    wut[:, wb + 128 * k:wb + 128 * (k + 1)],
                    xn_t[:, TOK * k + TAU * tq:TOK * k + TAU * (tq + 1)],
                    start=(k == 0), stop=(k == NCH - 1))),
                    inc=("PE" if k == NCH - 1 else None))
            pe_u = p.c("PE")
            if tq == NTQ - 1:
                wg_last[sl] = pe_u
                wu_last[sl] = pe_u
            ssl = u % 2
            ws = [("PE", pe_u)]
            if u >= 2:
                pu = u - 2
                ws.append(("VE", au_ev[(e_, pu // NTQ, pu % NTQ)]))
            elif e_ > 0:
                ws.append(("VE", au_ev[(e_ - 1, IKCH - 1, NTQ - 1)]))
            p.op("scalar", (lambda e, ssl=ssl, bg=bg: e.activation(
                s_sc[:, TAU * ssl:TAU * (ssl + 1)], bg[:, :], AF.Silu)),
                waits=ws, inc="AC")
            ac_u = p.c("AC")
            p.op("vector", (lambda e, ssl=ssl, ic=ic, tq=tq, bu_=bu_:
                            e.tensor_tensor(
                h_t[:, TOK * ic + TAU * tq:TOK * ic + TAU * (tq + 1)],
                bu_[:, :],
                s_sc[:, TAU * ssl:TAU * (ssl + 1)], ALU.mult)),
                waits=[("AC", ac_u)], inc="VE")
            au_ev[(e_, ic, tq)] = p.c("VE")
            if ig is not None:
                next(ig, None)
                next(ig, None)
        nic = ic + 2
        if nic < IKCH:
            wdma("g", e_, nic, wg_r, dWG[e_, nic], wg_last, gl_wg)
            wdma("u", e_, nic, wu_r, dWU[e_, nic], wu_last, gl_wu)
        elif e_ + 1 < NE:
            wdma("g", e_ + 1, nic - IKCH, wg_r, dWG[e_ + 1, nic - IKCH],
                 wg_last, gl_wg)
            wdma("u", e_ + 1, nic - IKCH, wu_r, dWU[e_ + 1, nic - IKCH],
                 wu_last, gl_wu)

    emit_prefetch2(0)
    emit_gb(0)

    p.op("vector", lambda e: e.tensor_copy(rowr[:, 0:32], R(6)),
         waits=[("VE", M["p1_done"])])
    p.op("vector", lambda e: e.tensor_copy(rowr[:, 32:64], R(7)), inc="VE")
    vpr = p.c("VE")
    p.op("tensor", lambda e: e.matmul(pb[6][0:8, 0:32], sorr[:, 0:8],
                                      rowr[:, 0:32], start=True, stop=True),
         waits=[("VE", vpr), ("VE", M["p2_done"])])
    p.op("tensor", lambda e: e.matmul(pb7f[0:8, 0:32], sorr[:, 0:8],
                                      rowr[:, 32:64], start=True, stop=True),
         inc="PE")
    M["rp8"] = p.c("PE")

    for _ in vrot_gen(1):
        pass

    if debug:
        p.op("gpsimd", lambda e: e.dma_start(out=dbg["dbg_u"], in_=sbu[:, :]),
             waits=[("VE", M[("scan", 1)])], inc="SO", amt=16)

    seqmark("scans")
    p.op("vector", lambda e: e.tensor_copy(ecol[0:32, 0:1],
                                           sbu[32:64, TOK - 1:TOK]),
         waits=[("VE", M[("scan", 1)])])
    p.op("vector", lambda e: e.tensor_copy(ecol[32:64, 0:1],
                                           sbu[0:32, TOK - 1:TOK]))
    p.op("vector", lambda e: e.tensor_mul(ecol[:, 1:2], sbu[:, TOK - 1:TOK],
                                          scos[:, HB - 1:HB]))
    p.op("vector", lambda e: e.tensor_mul(ecol[:, 2:3], ecol[:, 0:1],
                                          ssin[:, HB - 1:HB]))
    p.op("vector", lambda e: e.tensor_sub(ecol[:, 3:4], ecol[:, 1:2],
                                          ecol[:, 2:3]), inc="VE")
    M["ek"] = p.c("VE")
    p.op("gpsimd", lambda e: e.dma_start(out=dCCI, in_=ecol[:, 3:4],
                                         allow_slow_non_contiguous=True),
         waits=[("VE", M["ek"])], inc="CCL", amt=16)
    gl_e = p.c("CCL")
    seqmark("precc")
    p.op("gpsimd", lambda e: e.collective_compute(
        "AllGather", ALU.bypass, replica_groups=[list(range(8))],
        ins=[dCCI.opt()], outs=[dCCO.opt()]),
        waits=[("CCL", gl_e)], inc="CC", amt=1)
    p.op("gpsimd", lambda e: e.dma_start(out=carr[:, 0:64], in_=dCCO),
         waits=[("CC", 1)], inc="CCL", amt=16)
    M["ccload"] = p.c("CCL")
    if debug:
        p.op("gpsimd", lambda e: e.dma_start(out=dbg["dbg_ek"],
                                             in_=carr[:, 0:64]),
             waits=[("CCL", M["ccload"])], inc="SO", amt=16)

    seqmark("postcc")
    def carry_gen():
        wrp = [("PE", M["rp8"]), ("CCL", M["ccload"]), ("XL", M["p0_sync"])]
        p.op("vector", lambda e: e.tensor_scalar(carr[:, 64:96], pb[6][0:8, 0:32],
                                                 sdst[:, :], None, ALU.mult),
             waits=wrp)
        yield
        p.op("vector", lambda e: e.tensor_scalar(carr[:, 96:128],
                                                 pb7f[0:8, 0:32], sdst[:, :],
                                                 2048.0, ALU.mult, ALU.mult),
             inc="VE")
        yield
        vmag = p.c("VE")
        p.op("scalar", lambda e: e.activation(carr[:, 64:96], carr[:, 64:96],
                                              AF.Exp, scale=2048.0),
             waits=[("VE", vmag)], inc="AC")
        yield
        amag = p.c("AC")
        p.op("vector", lambda e: e.tensor_scalar(carr[:, 64:96], carr[:, 64:96],
                                                 smsk[:, :], None, ALU.mult),
             waits=[("AC", amag)])
        yield
        p.op("vector", lambda e: e.tensor_scalar_mul(carr[:, 128:160],
                                                     carr[:, 96:128], INV2PI))
        yield
        p.op("vector", lambda e: e.tensor_copy(carr[:, 160:192].bitcast(I32),
                                               carr[:, 128:160]))
        yield
        p.op("vector", lambda e: e.tensor_copy(carr[:, 128:160],
                                               carr[:, 160:192].bitcast(I32)))
        yield
        p.op("vector", lambda e: e.scalar_tensor_tensor(
            carr[:, 160:192], carr[:, 128:160], -TWOPI, carr[:, 96:128],
            ALU.mult, ALU.add))
        yield
        p.op("vector", lambda e: e.tensor_single_scalar(
            carr[:, 384:416], carr[:, 160:192], PIF, ALU.is_gt))
        yield
        p.op("vector", lambda e: e.scalar_tensor_tensor(
            carr[:, 160:192], carr[:, 384:416], -TWOPI, carr[:, 160:192],
            ALU.mult, ALU.add))
        yield
        p.op("vector", lambda e: e.tensor_single_scalar(
            carr[:, 384:416], carr[:, 160:192], -PIF, ALU.is_lt))
        yield
        p.op("vector", lambda e: e.scalar_tensor_tensor(
            carr[:, 160:192], carr[:, 384:416], TWOPI, carr[:, 160:192],
            ALU.mult, ALU.add))
        yield
        vth1 = p.c("VE")
        p.op("scalar", lambda e: e.activation(carr[:, 128:160],
                                              carr[:, 160:192], AF.Sin),
             waits=[("VE", vth1)], inc="AC")
        yield
        ath1 = p.c("AC")
        p.op("vector", lambda e: e.tensor_scalar_add(carr[:, 192:224],
                                                     carr[:, 96:128], PIHALF),
             waits=[("AC", ath1)])
        yield
        p.op("vector", lambda e: e.tensor_scalar_mul(carr[:, 224:256],
                                                     carr[:, 192:224], INV2PI))
        yield
        p.op("vector", lambda e: e.tensor_copy(carr[:, 160:192].bitcast(I32),
                                               carr[:, 224:256]))
        yield
        p.op("vector", lambda e: e.tensor_copy(carr[:, 224:256],
                                               carr[:, 160:192].bitcast(I32)))
        yield
        p.op("vector", lambda e: e.scalar_tensor_tensor(
            carr[:, 160:192], carr[:, 224:256], -TWOPI, carr[:, 192:224],
            ALU.mult, ALU.add))
        yield
        p.op("vector", lambda e: e.tensor_single_scalar(
            carr[:, 384:416], carr[:, 160:192], PIF, ALU.is_gt))
        yield
        p.op("vector", lambda e: e.scalar_tensor_tensor(
            carr[:, 160:192], carr[:, 384:416], -TWOPI, carr[:, 160:192],
            ALU.mult, ALU.add))
        yield
        p.op("vector", lambda e: e.tensor_single_scalar(
            carr[:, 384:416], carr[:, 160:192], -PIF, ALU.is_lt))
        yield
        p.op("vector", lambda e: e.scalar_tensor_tensor(
            carr[:, 160:192], carr[:, 384:416], TWOPI, carr[:, 160:192],
            ALU.mult, ALU.add))
        yield
        vth2 = p.c("VE")
        p.op("scalar", lambda e: e.activation(carr[:, 160:192],
                                              carr[:, 160:192], AF.Sin),
             waits=[("VE", vth2)], inc="AC")
        yield
        ath2 = p.c("AC")
        p.op("vector", lambda e: e.tensor_mul(carr[:, 192:224], carr[:, 64:96],
                                              carr[:, 160:192]),
             waits=[("AC", ath2)])
        yield
        p.op("vector", lambda e: e.tensor_mul(carr[:, 224:256], carr[:, 64:96],
                                              carr[:, 128:160]))
        yield
        p.op("vector", lambda e: e.tensor_mul(carr[:, 256:288], carr[:, 192:224],
                                              carr[:, 0:32]))
        yield
        p.op("vector", lambda e: e.tensor_mul(carr[:, 288:320], carr[:, 224:256],
                                              carr[:, 32:64]))
        yield
        p.op("vector", lambda e: e.tensor_sub(carr[:, 256:288], carr[:, 256:288],
                                              carr[:, 288:320]))
        yield
        p.op("vector", lambda e: e.tensor_mul(carr[:, 320:352], carr[:, 192:224],
                                              carr[:, 32:64]))
        yield
        p.op("vector", lambda e: e.tensor_mul(carr[:, 288:320], carr[:, 224:256],
                                              carr[:, 0:32]))
        yield
        p.op("vector", lambda e: e.tensor_add(carr[:, 288:320], carr[:, 288:320],
                                              carr[:, 320:352]), inc="VE")
        yield
        M["pprod"] = p.c("VE")
        p.op("tensor", lambda e: e.matmul(pb[6][0:1, 0:32], so8[:, :],
                                          carr[:, 256:288], start=True,
                                          stop=True),
             waits=[("VE", M["pprod"])])
        yield
        p.op("tensor", lambda e: e.matmul(pb7f[0:1, 0:32], so8[:, :],
                                          carr[:, 288:320], start=True,
                                          stop=True), inc="PE")
        yield
        M["csum"] = p.c("PE")
        p.op("vector", lambda e: e.tensor_copy(prow[:, 352:384],
                                               pb[6][0:1, 0:32]),
             waits=[("PE", M["csum"])])
        yield
        p.op("vector", lambda e: e.tensor_copy(prow[:, 384:416],
                                               pb7f[0:1, 0:32]), inc="VE")
        yield
        M["crows"] = p.c("VE")
        p.op("tensor", lambda e: e.transpose(pb[6][0:32, 0:1],
                                             prow[:, 352:384], sid1[:, :]),
             waits=[("VE", M["crows"])])
        yield
        p.op("tensor", lambda e: e.transpose(pb7f[0:32, 0:1],
                                             prow[:, 384:416], sid1[:, :]),
             inc="PE")
        yield
        M["ctp"] = p.c("PE")
        p.op("vector", lambda e: e.tensor_copy(ecol[0:32, 0:1],
                                               pb[6][0:32, 0:1]),
             waits=[("PE", M["ctp"])])
        yield
        p.op("vector", lambda e: e.tensor_copy(ecol[0:32, 1:2],
                                               pb7f[0:32, 0:1]))
        yield
        p.op("vector", lambda e: e.tensor_mul(ecol[0:32, 2:3], C(6),
                                              ecol[0:32, 0:1]))
        yield
        p.op("vector", lambda e: e.tensor_mul(ecol[0:32, 3:4], C(7),
                                              ecol[0:32, 1:2]))
        yield
        p.op("vector", lambda e: e.tensor_sub(uinc[0:32, 0:1], ecol[0:32, 2:3],
                                              ecol[0:32, 3:4]))
        yield
        p.op("vector", lambda e: e.tensor_mul(ecol[0:32, 2:3], C(6),
                                              ecol[0:32, 1:2]))
        yield
        p.op("vector", lambda e: e.tensor_mul(ecol[0:32, 3:4], C(7),
                                              ecol[0:32, 0:1]))
        yield
        p.op("vector", lambda e: e.tensor_add(ecol[0:32, 2:3], ecol[0:32, 2:3],
                                              ecol[0:32, 3:4]))
        yield
        p.op("vector", lambda e: e.tensor_copy(uinc[32:64, 0:1],
                                               ecol[0:32, 2:3]), inc="VE")
        yield
        M["uinit"] = p.c("VE")

    cg = carry_gen()
    emit_aic(0, 0)
    emit_aic(0, 1)
    emit_aic(0, 2, ig=cg)
    emit_aic(0, 3, ig=cg)

    seqmark("carrydone")
    def xsphase_gen(hf):
        hoff = HB * hf
        yield from tables_gen(hf, ssgx[:, :], [("VE", M["uinit"])])
        tdone = M[("tdone", hf)]
        p.op("vector", lambda e: e.tensor_scalar(ssc5[:, :], sswp[:, :],
                                                 1.0, RHOD, ALU.add,
                                                 ALU.mult),
             waits=[("VE", tdone)], inc="VE")
        vr = p.c("VE")
        p.op("scalar", lambda e: e.activation(sswp[:, :], ssc5[:, :],
                                              AF.Exp),
             waits=[("VE", vr)], inc="AC")
        ar = p.c("AC")
        yield
        p.op("vector", lambda e: e.tensor_scalar(ssc5[:, :], sswp[:, :],
                                                 uinc[:, :], None, ALU.mult),
             waits=[("AC", ar)])
        yield
        p.op("vector", (lambda e, hoff=hoff: e.tensor_add(
            sbu[:, hoff:hoff + HB], sbu[:, hoff:hoff + HB], ssc5[:, :])))
        yield
        p.op("vector", (lambda e, hoff=hoff: e.tensor_copy(
            sswp[0:32, :], sbu[32:64, hoff:hoff + HB])))
        yield
        p.op("vector", (lambda e, hoff=hoff: e.tensor_copy(
            sswp[32:64, :], sbu[0:32, hoff:hoff + HB])))
        yield
        p.op("vector", (lambda e, hoff=hoff: e.tensor_mul(
            svv[:, :], sbu[:, hoff:hoff + HB], scos[:, :])))
        yield
        p.op("vector", lambda e: e.tensor_mul(sswp[:, :], sswp[:, :],
                                              ssin[:, :]))
        yield
        p.op("vector", (lambda e, hoff=hoff: e.tensor_add(
            xs_t[:, hoff:hoff + HB], svv[:, :], sswp[:, :])), inc="VE")
        M[("xs", hf)] = p.c("VE")

    emit_aic(0, 4, ig=cg)
    emit_aic(0, 5, ig=cg)
    emit_aic(0, 6, ig=cg)
    emit_aic(0, 7, ig=cg)
    for _ in cg:
        pass

    if debug:
        p.op("gpsimd", lambda e: e.dma_start(out=dbg["dbg_carr"],
                                             in_=carr[:, :]),
             waits=[("VE", M["uinit"])], inc="SO", amt=16)
        p.op("gpsimd", lambda e: e.dma_start(out=dbg["dbg_uin"],
                                             in_=uinc[:, :],
                                             allow_slow_non_contiguous=True),
             waits=[("VE", M["uinit"])], inc="SO", amt=16)

    def xs_all_gen():
        yield from xsphase_gen(0)
        yield from xsphase_gen(1)

    def emit_p3flush(xsteps):
        for _ in xsteps:
            pass
        M["p3_done"] = M[("xs", 1)]
        if debug:
            p.op("vector", lambda e: e.tensor_copy(sswp[:, :],
                                                   xs_t[:, 0:HB]),
                 waits=[("VE", M["p3_done"])])
            p.op("vector", lambda e: e.tensor_copy(svv[:, :],
                                                   xs_t[:, HB:TOK]),
                 inc="VE")
            vd = p.c("VE")
            p.op("gpsimd", lambda e: e.dma_start(out=dbg["dbg_xs"][:, 0:HB],
                                                 in_=sswp[:, :]),
                 waits=[("VE", vd)], inc="SO", amt=16)
            p.op("gpsimd", lambda e: e.dma_start(out=dbg["dbg_xs"][:, HB:TOK],
                                                 in_=svv[:, :]),
                 inc="SO", amt=16)

    seqmark("p3end")

    y2ev = {}

    def emit_y2():
        for dc in range(NCH):
            for tq in range(NTQ):
                u = dc * NTQ + tq
                bk = pb[6] if u % 2 == 0 else pb7f
                w = [("VE", M["p3_done"]), ("VE", M["cstack"])]
                if u >= 2:
                    w.append(("VE", y2ev[u - 2]))
                else:
                    w.append(("VE", bu_ev[(0, NCH - 1, NTQ - 1)]))
                p.op("tensor", (lambda e, dc=dc, tq=tq, bk=bk: e.matmul(
                    bk[:, :], cstk[:, 128 * dc:128 * (dc + 1)],
                    xs_t[:, TAU * tq:TAU * (tq + 1)], start=True,
                    stop=True)),
                    waits=w, inc="PE")
                pe_y = p.c("PE")
                gp = gpair[0]
                gslot = gp % 4
                ghalf = u % 2
                we = [("PE", pe_y)]
                if gp >= 4 and (gp - 4) in acst:
                    we.append(acst[gp - 4])
                p.op("vector", (lambda e, gslot=gslot, ghalf=ghalf, bk=bk:
                                e.tensor_copy(
                    gsc[:, HB * gslot + TAU * ghalf:
                        HB * gslot + TAU * (ghalf + 1)], bk[:, :])),
                    waits=we, inc="VE")
                y2ev[u] = p.c("VE")
                if ghalf == 1:
                    tqp = tq // 2
                    p.op("gpsimd",
                         (lambda e, dc=dc, tqp=tqp, gslot=gslot:
                          e.dma_start(
                        out=dOUT[dc, tqp],
                        in_=gsc[:, HB * gslot:HB * (gslot + 1)],
                        accum_op=ALU.add)),
                        waits=[("VE", y2ev[u])] +
                        ([region_so[(dc, tqp)]] if (dc, tqp) in region_so
                         else []),
                        inc="SG%d" % gslot, amt=16)
                    acst[gp] = ("SG%d" % gslot, p.c("SG%d" % gslot))
                    region_so[(dc, tqp)] = acst[gp]
                    gpair[0] += 1

    for e_ in range(NE):
        for ic in range(IKCH if e_ == 0 else 0, IKCH):
            emit_aic(e_, ic)
            if e_ == 1 and ic == 1:
                emit_y2()

        if e_ + 1 < NE:
            emit_gb(e_ + 1)
        gbuf = g_sb if e_ % 2 == 0 else g_sbo
        xsteps = xs_all_gen() if e_ == 0 else None

        for dc in range(NCH):
            ndc = dc + 2
            if ndc < NCH:
                wdma("d", e_, ndc, wd_r, dWD[e_, ndc], wd_last, gl_wd)
            elif e_ + 1 < NE:
                wdma("d", e_ + 1, ndc - NCH, wd_r, dWD[e_ + 1, ndc - NCH],
                     wd_last, gl_wd)
            for tq in range(NTQ):
                u = dc * NTQ + tq
                bk = pb[6] if u % 2 == 0 else pb7f
                w = [gl_wd[e_ * NCH + dc],
                     ("VE", au_ev[(e_, IKCH - 1, NTQ - 1)])]
                if u >= 2:
                    pu = u - 2
                    w.append(("VE", bu_ev[(e_, pu // NTQ, pu % NTQ)]))
                elif e_ == 0:
                    w.append(("VE", M["uinit"]))
                elif e_ == 1:
                    w.append(("VE", y2ev[NCH * NTQ - 1]))
                else:
                    w.append(("VE", bu_ev[(e_ - 1, NCH - 1, NTQ - 1)]))
                sl = (e_ * NCH + dc) % 3
                for ik in range(IKCH):
                    p.op("tensor", (lambda e, ik=ik, tq=tq, sl=sl, bk=bk:
                                    e.matmul(
                        bk[:, :],
                        wd_r[:, 1024 * sl + 128 * ik:1024 * sl + 128 * (ik + 1)],
                        h_t[:, TOK * ik + TAU * tq:TOK * ik + TAU * (tq + 1)],
                        start=(ik == 0), stop=(ik == IKCH - 1))),
                        waits=(w if ik == 0 else []),
                        inc=("PE" if ik == IKCH - 1 else None))
                pe_b = p.c("PE")
                if tq == NTQ - 1:
                    wd_last[sl] = pe_b
                gp = gpair[0]
                gslot = gp % 4
                ghalf = u % 2
                we = [("PE", pe_b)]
                if gp >= 4 and (gp - 4) in acst:
                    we.append(acst[gp - 4])
                p.op("vector", (lambda e, gslot=gslot, ghalf=ghalf, bk=bk,
                                tq=tq, gbuf=gbuf: e.tensor_tensor(
                    gsc[:, HB * gslot + TAU * ghalf:
                        HB * gslot + TAU * (ghalf + 1)], bk[:, :],
                    gbuf[:, TAU * tq:TAU * (tq + 1)], ALU.mult)),
                    waits=we, inc="VE")
                bu_ev[(e_, dc, tq)] = p.c("VE")
                if ghalf == 1:
                    tqp = tq // 2
                    p.op("gpsimd",
                         (lambda e, dc=dc, tqp=tqp, gslot=gslot:
                          e.dma_start(
                        out=dOUT[dc, tqp],
                        in_=gsc[:, HB * gslot:HB * (gslot + 1)],
                        accum_op=ALU.add)),
                        waits=[("VE", bu_ev[(e_, dc, tq)])] +
                        ([region_so[(dc, tqp)]] if (dc, tqp) in region_so
                         else []),
                        inc="SG%d" % gslot, amt=16)
                    acst[gp] = ("SG%d" % gslot, p.c("SG%d" % gslot))
                    region_so[(dc, tqp)] = acst[gp]
                    gpair[0] += 1
                if xsteps is not None:
                    next(xsteps, None)

        if e_ == 0:
            emit_p3flush(xsteps)

        if debug and e_ == NE - 1:
            p.op("gpsimd", lambda e: e.dma_start(out=dbg["dbg_h"],
                                                 in_=h_t[:, :]),
                 waits=[("VE", au_ev[(NE - 1, IKCH - 1, NTQ - 1)])],
                 inc="SO", amt=16)

    final_waits = [(s, p.c(s)) for s in
                   ("SG0", "SG1", "SG2", "SG3", "SI0", "SI1")
                   if p.c(s) > 0]
    if p.c("SO") > 0:
        final_waits.append(("SO", p.c("SO")))
    p.op("gpsimd", lambda e: e.engine_nop(), waits=final_waits)

    stack = contextlib.ExitStack()
    sems = {}
    for s in ("XL", "XC", "XI", "XA0", "XA1", "WG0", "WG1", "WU0", "WU1", "WD0", "WD1", "WD2", "SG0", "SG1", "SG2", "SG3", "SI0", "SI1", "XR0", "XR1", "CCL", "GL", "SO", "CC", "PE", "VE", "AC", "EW"):
        sems[s] = stack.enter_context(nc.semaphore(s))

    import os
    trunc = int(os.environ.get("KERNEL_TRUNC", "0")) or None
    maxcnt = {}
    if trunc is not None:
        for nm in ENGS:
            for waits, fn, inc, amt, seq in p.q[nm]:
                if seq < trunc and inc is not None:
                    maxcnt[inc] = maxcnt.get(inc, 0) + amt

    def replay(name):
        def go(eng):
            hw = {}
            for waits, fn, inc, amt, seq in p.q[name]:
                if trunc is not None and seq >= trunc:
                    continue
                for s, v in waits:
                    if trunc is not None:
                        v = min(v, maxcnt.get(s, 0))
                    if v > 0 and hw.get(s, -1) < v:
                        eng.wait_ge(sems[s], v)
                        hw[s] = v
                inst = fn(eng)
                if inc is not None:
                    inst.then_inc(sems[inc], amt)
        return go

    with stack:
        with nc.Block() as block:
            block.sync(replay("sync"))
            block.gpsimd(replay("gpsimd"))
            block.vector(replay("vector"))
            block.scalar(replay("scalar"))
            block.tensor(replay("tensor"))
    return nc


def _layout(u, lre, lim, bre, bim, cre, cim, dvec, logd, rg, wgu, wdn):
    f32 = np.float32
    toks = np.ascontiguousarray(u.reshape(16384, 2048))
    xs = [np.ascontiguousarray(
        toks[2048 * k:2048 * (k + 1)].T.reshape(16, 128, 2, 1024)
        .transpose(0, 2, 1, 3)) for k in range(8)]

    gate = wgu[:, :, 0::2]
    up = wgu[:, :, 1::2]
    wg_b = np.ascontiguousarray(
        gate.reshape(8, 16, 128, 8, 128).transpose(0, 3, 2, 1, 4))
    wu_b = np.ascontiguousarray(
        up.reshape(8, 16, 128, 8, 128).transpose(0, 3, 2, 1, 4))
    wd_b = np.ascontiguousarray(
        wdn.reshape(8, 8, 128, 16, 128).transpose(0, 3, 2, 1, 4))
    rgt = np.ascontiguousarray(
        rg.T.reshape(16, 128, 8).transpose(1, 0, 2).reshape(128, 128))

    prm = np.zeros((1, 96), f32)
    prm[0, 0:32] = lre
    prm[0, 32:64] = lim
    prm[0, 64:96] = logd[:32]
    dv = np.ascontiguousarray(dvec.reshape(16, 128).T)
    iota = np.ascontiguousarray(
        np.tile(np.arange(2048, dtype=f32), (64, 1))
        .reshape(64, 2, 1024).transpose(1, 0, 2))
    oh = np.zeros((8, 8 * 128), f32)
    for e in range(8):
        oh[e, 128 * e:128 * (e + 1)] = 1.0
    sgnv = np.concatenate([np.ones((32, 1), f32), -np.ones((32, 1), f32)])
    cvec = np.zeros((128, 8), f32)
    cvec[:, 0] = 1e-5
    cvec[:, 1] = 1.0
    cvec[:, 2] = float(np.pi / 2.0)

    base = {
        "wg": wg_b, "wu": wu_b, "wd": wd_b, "rgt": rgt,
        "bre": np.ascontiguousarray(bre),
        "bim": np.ascontiguousarray(bim),
        "cret": np.ascontiguousarray(cre.T),
        "cimt": np.ascontiguousarray(cim.T),
        "prm": prm, "dvec": dv, "iota64": iota,
        "id128": np.eye(128, dtype=f32), "id8": np.eye(8, dtype=f32),
        "id1": np.ones((1, 1), f32),
        "id32b": np.concatenate([np.eye(32), np.eye(32)], 0).astype(ml_dtypes.bfloat16),
        "onesr_col": np.ones((128, 1), f32),
        "onesr_row": np.ones((1, 128), f32),
        "ones8": np.ones((8, 1), f32),
        "oh": oh, "sgnv": sgnv, "sgnx": np.ascontiguousarray(-sgnv),
        "cvec": cvec,
    }
    in_maps = []
    for k in range(8):
        m = dict(base)
        m["x_fm"] = xs[k]
        dist = np.zeros((8, 1), f32)
        mask = np.zeros((8, 1), f32)
        seq = k // 4
        for kp in range(8):
            if kp // 4 == seq and kp < k:
                mask[kp, 0] = 1.0
                dist[kp, 0] = float(k - 1 - kp)
        m["dist8"] = dist
        m["mask8"] = mask
        in_maps.append(m)
    return in_maps


_NC_CACHE = {}


def kernel(L0_attn_u, L0_attn_Lambda_unconstrained_re, L0_attn_Lambda_im,
           L0_attn_B_re, L0_attn_B_im, L0_attn_C_re, L0_attn_C_im, L0_attn_D,
           L0_attn_log_Delta, L0_ffn_router_gate, L0_ffn_expert_gate_up,
           L0_ffn_expert_down):
    in_maps = _layout(np.asarray(L0_attn_u, np.float32),
                      np.asarray(L0_attn_Lambda_unconstrained_re, np.float32),
                      np.asarray(L0_attn_Lambda_im, np.float32),
                      np.asarray(L0_attn_B_re, np.float32),
                      np.asarray(L0_attn_B_im, np.float32),
                      np.asarray(L0_attn_C_re, np.float32),
                      np.asarray(L0_attn_C_im, np.float32),
                      np.asarray(L0_attn_D, np.float32),
                      np.asarray(L0_attn_log_Delta, np.float32),
                      np.asarray(L0_ffn_router_gate, np.float32),
                      np.asarray(L0_ffn_expert_gate_up, np.float32),
                      np.asarray(L0_ffn_expert_down, np.float32))
    if "nc" not in _NC_CACHE:
        _NC_CACHE["nc"] = build_nc(debug=False)
    nc = _NC_CACHE["nc"]
    res = run_bass_kernel_spmd(nc, in_maps, core_ids=list(range(8)))
    shards = [np.asarray(r["out_fm"]).transpose(0, 2, 1, 3)
              .reshape(2048, 2048).T for r in res.results]
    out = np.concatenate(shards, axis=0).reshape(2, 8192, 2048)
    return np.ascontiguousarray(out, dtype=np.float32)
